# version 10
# baseline (speedup 1.0000x reference)
import contextlib
import numpy as np
import concourse.bass as bass
import concourse.mybir as mybir
from concourse.bass_utils import run_bass_kernel_spmd

F32 = mybir.dt.float32
BF16 = mybir.dt.bfloat16
AF = mybir.ActivationFunctionType
ALU = mybir.AluOpType
AX = mybir.AxisListType.X

T, D, INNER, DS, DTR = 4096, 1024, 2048, 16, 64
TC = 512
NCH = T // TC
NTG = TC // 128
NDT = INNER // 128
KD = D // 128
NE = 8



GLANES = ["G%d" % i for i in range(8)]
XLANES = ["X%d" % i for i in range(4)]
DLANES = {"gpsimd": GLANES, "sync": XLANES}
SEMKEYS = ["T", "V", "S", "P"] + GLANES + XLANES


class P:

    def __init__(self):
        self.ops = {k: [] for k in ("tensor", "vector", "scalar", "sync", "gpsimd")}
        self.n = {k: 0 for k in SEMKEYS}
        self.rr = {"gpsimd": 0, "sync": 0}

    def emit(self, eng, fn, inckey=None):
        if eng == "vector":
            inckey = "V"
        elif eng == "scalar":
            inckey = "S"
        self.ops[eng].append((fn, inckey))
        if inckey is not None:
            return self.inc(inckey)
        return None

    def wait(self, eng, sem_key, val):
        if val and val > 0:
            self.ops[eng].append(("wait", sem_key, val))

    def wtok(self, eng, tok):
        if tok:
            for lane, val in tok.items():
                self.wait(eng, lane, val)

    def dma(self, eng, dst, src):
        lanes = DLANES[eng]
        i = self.rr[eng]
        self.rr[eng] = (i + 1) % len(lanes)
        lane = lanes[i]
        self.wait(eng, lane, self.n[lane])
        self.ops[eng].append((lambda e, dst=dst, src=src: e.dma_start(dst, src), lane))
        self.n[lane] += 16
        return {lane: self.n[lane]}

    def inc(self, sem_key):
        self.n[sem_key] += 16 if len(sem_key) > 1 else 1
        return self.n[sem_key]


def tokmerge(*toks):
    d = {}
    for t in toks:
        for lane, v in t.items():
            d[lane] = max(d.get(lane, 0), v)
    return d


def build_core():
    nc = bass.Bass("TRN2", target_bir_lowering=False)

    def din(name, shape):
        return nc.dram_tensor(name, list(shape), F32, kind="ExternalInput").ap()

    x_h = din("x", (T, D))
    ipw_h = din("ipw", (D, 2 * INNER))
    ipb_h = din("ipb", (128, 32))
    opw_h = din("opw", (INNER, D))
    opbr_h = din("opbr", (1, D))
    dtiw_h = din("dtiw", (128, NDT * DTR))
    dtib_h = din("dtib", (128, 1))
    dtow_h = din("dtow", (128, 1024))
    dtob_h = din("dtob", (128, NDT))
    dtbb_h = din("dtbb", (128, NDT))
    bpw_h = din("bpw", (128, NDT * DS))
    bpb_h = din("bpb", (DS, 1))
    cpw_h = din("cpw", (128, NDT * DS))
    cpb_h = din("cpb", (DS, 1))
    ck_h = din("ck", (128, NDT * 4))
    cb_h = din("cb", (128, NDT))
    alog_h = din("alog", (128, NDT * DS))
    dd_h = din("dd", (128, NDT))
    ns_h = din("ns", (128, KD))
    gw_h = din("gw", (128, KD * NE))
    w1t_h = din("w1t", (NE, D, 1536))
    w2t_h = din("w2t", (NE, 768, D))
    ws1t_h = din("ws1t", (D, 3072))
    ws2t_h = din("ws2t", (1536, D))
    id_h = din("ident", (128, 128))
    sel_h = din("sel", (DS, DS * 128))
    out_h = nc.dram_tensor("out", [T, D], F32, kind="ExternalOutput").ap()

    p = P()
    with contextlib.ExitStack() as ctx:
        ec = ctx.enter_context
        sem = {k: ec(nc.semaphore("sem" + k)) for k in SEMKEYS}

        A = ec(nc.sbuf_tensor("A", [128, 24576], BF16))
        B = ec(nc.sbuf_tensor("B", [128, 6144], BF16))
        xsb = ec(nc.sbuf_tensor("xsb", [128, NTG * D], F32))
        XA1 = ec(nc.sbuf_tensor("XA1", [128, D], F32))
        u_pre = ec(nc.sbuf_tensor("u_pre", [128, NDT * 516], BF16))
        szygz = ec(nc.sbuf_tensor("szygz", [128, NDT * 512], BF16))
        xn_fmA = ec(nc.sbuf_tensor("xn_fmA", [128, KD * 512], BF16))
        xn_fmB = ec(nc.sbuf_tensor("xn_fmB", [128, KD * 512], BF16))
        xn_tm = ec(nc.sbuf_tensor("xn_tm", [128, D], BF16))
        S1 = ec(nc.sbuf_tensor("S1", [128, D], F32))
        st = ec(nc.sbuf_tensor("st", [128, 24], F32))
        carry = ec(nc.sbuf_tensor("carry", [128, NDT * DS], F32))
        hst = ec(nc.sbuf_tensor("hst", [128, NDT * 3], BF16))
        dt_t = ec(nc.sbuf_tensor("dt_t", [128, 512], F32))
        yac2 = ec(nc.sbuf_tensor("yac2", [128, 1024], F32))
        w_t = ec(nc.sbuf_tensor("w_t", [128, 512], BF16))
        db2 = ec(nc.sbuf_tensor("db2", [128, 1024], BF16))
        db3 = ec(nc.sbuf_tensor("db3", [128, 2048], BF16))
        h2 = ec(nc.sbuf_tensor("h2", [128, 1024], BF16))
        h2b = ec(nc.sbuf_tensor("h2b", [128, 1024], BF16))
        sg_t = ec(nc.sbuf_tensor("sg_t", [128, 512], BF16))
        wb_e = ec(nc.sbuf_tensor("wb_e", [128, 512], BF16))
        w2tmp = ec(nc.sbuf_tensor("w2tmp", [128, 1024], F32))
        dtr_sb = ec(nc.sbuf_tensor("dtr_sb", [128, 512], BF16))
        b_sb = ec(nc.sbuf_tensor("b_sb", [DS, 512], BF16))
        c_sb = ec(nc.sbuf_tensor("c_sb", [DS, 512], BF16))
        wf = ec(nc.sbuf_tensor("wf", [NE, 512], BF16))
        sel_sb = ec(nc.sbuf_tensor("sel_sb", [DS, DS * 128], BF16))
        w_tm = ec(nc.sbuf_tensor("w_tm", [128, NTG * NE], BF16))
        gsc = ec(nc.sbuf_tensor("gsc", [128, NTG * NE], F32))
        gmask = ec(nc.sbuf_tensor("gmask", [128, NTG * NE], F32))
        ones1 = ec(nc.sbuf_tensor("ones1", [1, 128], BF16))
        id_bf = ec(nc.sbuf_tensor("id_bf", [128, 128], BF16))
        opb_sb = ec(nc.sbuf_tensor("opb_sb", [1, D], BF16))
        ipb_sb = ec(nc.sbuf_tensor("ipb_sb", [128, 32], F32))
        dtiw_sb = ec(nc.sbuf_tensor("dtiw_sb", [128, NDT * DTR], BF16))
        dtib_sb = ec(nc.sbuf_tensor("dtib_sb", [128, 1], F32))
        dtob_sb = ec(nc.sbuf_tensor("dtob_sb", [128, NDT], F32))
        dtbb_sb = ec(nc.sbuf_tensor("dtbb_sb", [128, NDT], F32))
        bpw_sb = ec(nc.sbuf_tensor("bpw_sb", [128, NDT * DS], BF16))
        bpb_sb = ec(nc.sbuf_tensor("bpb_sb", [DS, 1], F32))
        cpw_sb = ec(nc.sbuf_tensor("cpw_sb", [128, NDT * DS], BF16))
        cpb_sb = ec(nc.sbuf_tensor("cpb_sb", [DS, 1], F32))
        ck_sb = ec(nc.sbuf_tensor("ck_sb", [128, NDT * 4], F32))
        cb_sb = ec(nc.sbuf_tensor("cb_sb", [128, NDT], F32))
        a_sb = ec(nc.sbuf_tensor("a_sb", [128, NDT * DS], F32))
        dd_sb = ec(nc.sbuf_tensor("dd_sb", [128, NDT], F32))
        ns_sb = ec(nc.sbuf_tensor("ns_sb", [128, KD], F32))
        gw_sb = ec(nc.sbuf_tensor("gw_sb", [128, KD * NE], BF16))
        resv = ec(nc.sbuf_tensor("resv", [128, 16384], BF16))
        raddr = nc.lookup_mloc(resv).addr
        ffn_acc = nc.alloc_sbuf_tensor_at("ffn_acc", [128, NTG * D], F32, offset=raddr)
        act = nc.alloc_sbuf_tensor_at("act", [128, 12 * 512], BF16, offset=raddr + 16384)
        XA0 = nc.alloc_sbuf_tensor_at("XA0", [128, D], F32, offset=raddr + 28672)
        XAp = [XA0, XA1]
        psA = ec(nc.psum_tensor("psA", [128, 512], F32))
        psB = ec(nc.psum_tensor("psB", [128, 512], F32))
        psC = ec(nc.psum_tensor("psC", [128, 512], F32))
        psD = ec(nc.psum_tensor("psD", [128, 512], F32))
        psE = ec(nc.psum_tensor("psE", [128, 512], F32))
        psF = ec(nc.psum_tensor("psF", [128, 512], F32))
        psG = ec(nc.psum_tensor("psG", [128, 1024], BF16))
        psH = ec(nc.psum_tensor("psH", [128, 512], F32))
        psAB = [psA, psB]
        psABCD = [psA, psB, psC, psD]
        psCDE = [psC, psD, psE]
        psEF = [psE, psF]
        psCD = [psC, psD]

        MEAN, INV, NEGMI = 0, 4, 8
        C1, C2, C3, C4, C5, C6, C7 = 12, 13, 14, 15, 16, 17, 18
        KEPS, KONE, KNHALF, KNEG1 = 19, 20, 21, 22

        def col(i):
            return st[:, i:i + 1]

        def g_dma(dst, src):
            return p.dma("gpsimd", dst, src)

        def pmit(fn):
            p.wait("gpsimd", "P", p.n["P"])
            return p.emit("gpsimd", fn, "P")

        def s_dma(dst, src):
            return p.dma("sync", dst, src)

        def lnstats(invcol, negcol, sumcol, sqcol):
            p.emit("vector", lambda e: e.tensor_scalar(col(C3), col(sumcol), 1.0 / D,
                                                       None, ALU.mult), "V")
            p.emit("vector", lambda e: e.tensor_tensor(col(C4), col(C3), col(C3),
                                                       ALU.mult), "V")
            p.emit("vector", lambda e: e.tensor_scalar(col(C5), col(sqcol), 1.0 / D,
                                                       None, ALU.mult), "V")
            v_var = p.emit("vector", lambda e: e.tensor_tensor(col(C5), col(C5), col(C4),
                                                               ALU.subtract), "V")
            p.wait("scalar", "V", v_var)
            p.emit("scalar", lambda e: e.activation(col(C5), col(C5), AF.Ln,
                                                    bias=col(KEPS)))
            s_sd = p.emit("scalar", lambda e, ic=invcol: e.activation(
                col(ic), col(C5), AF.Exp, scale=col(KNHALF)), "S")
            p.wait("vector", "S", s_sd)
            v_i = p.emit("vector", lambda e, ic=invcol, ng=negcol: e.scalar_tensor_tensor(
                col(ng), col(C3), -1.0, col(ic), ALU.mult, ALU.mult), "V")
            return v_i

        resb = {}
        for dst, src in ((dtiw_sb[:], dtiw_h),
                         (bpw_sb[:], bpw_h), (cpw_sb[:], cpw_h),
                         (gw_sb[:], gw_h), (id_bf[:], id_h), (opb_sb[:], opbr_h),
                         (sel_sb[:], sel_h)):
            resb = tokmerge(resb, g_dma(dst, src))
        resx = {}
        for dst, src in ((ipb_sb[:], ipb_h), (dtib_sb[:], dtib_h),
                         (dtob_sb[:], dtob_h), (dtbb_sb[:], dtbb_h),
                         (bpb_sb[:], bpb_h), (cpb_sb[:], cpb_h),
                         (ck_sb[:], ck_h), (cb_sb[:], cb_h),
                         (a_sb[:], alog_h), (dd_sb[:], dd_h), (ns_sb[:], ns_h)):
            resx = tokmerge(resx, s_dma(dst, src))
        p.wtok("vector", resx)
        p.wtok("vector", resb)
        p.emit("vector", lambda e: e.memset(ones1[:], 1.0))
        p.emit("vector", lambda e: e.memset(carry[:], 0.0))
        p.emit("vector", lambda e: e.memset(u_pre[:], 0.0))
        p.emit("vector", lambda e: e.memset(col(KEPS), 1e-5))
        p.emit("vector", lambda e: e.memset(col(KONE), 1.0))
        p.emit("vector", lambda e: e.memset(col(KNHALF), -0.5))
        p.emit("vector", lambda e: e.memset(col(KNEG1), -1.0))
        vinit = p.emit("vector", lambda e: e.tensor_add(dtob_sb[:], dtob_sb[:],
                                                        dtbb_sb[:]), "V")
        p.wait("scalar", "V", vinit)
        p.emit("scalar", lambda e: e.activation(a_sb[:], a_sb[:], AF.Exp))
        ainit = p.emit("scalar", lambda e: e.activation(a_sb[:], a_sb[:], AF.Identity,
                                                        scale=col(KNEG1)), "S")

        uread = [0] * NDT
        ygz_done = 0
        lastW2mm = 0
        lastWs2mm = 0
        outproj_done = 0
        ffn_last_read = 0
        xn_fm_free = 0
        evacT = [0]
        scan_v = 0
        out_tok = [None] * NTG
        sgread = 0
        w2tmp_free = [0, 0]
        ffn_pool_done = 0

        IPW0, W1R0, BB0, CB0 = 16384, 20480, 0, 8192
        ipw_free = [0] * 4
        ring_free = [0] * 4
        ring_slotof = {}
        rctr2 = [0]
        ipw_tok = [None] * 32
        xa_vfree = [0, 0]
        xa_sfree = [0, 0]
        live = {"s_dtr": 0, "w2_ld0": {}, "bball_rd": 0, "bball_pool": 0}
        cd = {"free": 0}
        psab_last = [("V", 0), ("V", 0)]
        xn_fm_free = 0
        def make_ffn(xn_fm, w2_ld, wfev_all, p8s, ygz, ring_gather,
                     w1_tok, ws1_tok, ws1_slot, ring_slotof, nchunk, is_last):
            nonlocal lastW2mm, lastWs2mm, ffn_last_read, sgread, xn_fm_free
            nonlocal ffn_pool_done
            p.wait("tensor", "S", live["s_dtr"])
            for _ in range(4):
                ring_gather()
            actev = [0] * 6
            actw_done = [0] * 6
            w2acc = [0] * 8
            w1mm_done = [0] * NE
            for e_i in range(NE):
                p.wait("tensor", "S", wfev_all)
                p.wait("tensor", "S", cd["free"])
                if e_i > 0:
                    p.wait("tensor", "P", actw_done[5])
                p.emit("tensor", lambda e, e_i=e_i: e.matmul(
                    psC[:], sel_sb[0:NE, e_i * 128:(e_i + 1) * 128], wf[:],
                    start=True, stop=True), "T")
                t_wb = p.n["T"]
                p.wait("scalar", "T", t_wb)
                if e_i > 0:
                    p.wait("scalar", "P", actw_done[5])
                s_wb = p.emit("scalar", lambda e: e.activation(wb_e[:], psC[:], AF.Copy), "S")
                cd["free"] = s_wb
                for m in range(12):
                    ps = psAB[m % 2]
                    sl = ring_slotof[(e_i, m)]
                    if m >= 2:
                        if m - 2 < 6:
                            p.wait("tensor", "S", actev[m - 2])
                        else:
                            p.wait("tensor", "P", actw_done[m - 8])
                    elif e_i > 0:
                        p.wait("tensor", "S", w2acc[6 + m])
                    p.wtok("tensor", w1_tok[e_i][m])
                    for k in range(KD):
                        p.emit("tensor", lambda e, ps=ps, sl=sl, m=m, k=k,
                               xn_fm=xn_fm: e.matmul(
                            ps[:], A[:, W1R0 + sl * 1024 + k * 128:
                                     W1R0 + sl * 1024 + (k + 1) * 128],
                            xn_fm[:, k * 512:(k + 1) * 512],
                            start=(k == 0), stop=(k == KD - 1)), "T" if k == KD - 1 else None)
                    mm = p.n["T"]
                    ring_free[sl] = mm
                    ring_gather()
                    p.wait("scalar", "T", mm)
                    if m < 6:
                        actev[m] = p.emit("scalar", lambda e, ps=ps, m=m: e.activation(
                            act[:, m * 512:(m + 1) * 512], ps[:], AF.Copy), "S")
                    else:
                        p.wait("scalar", "P", sgread)
                        s_sg = p.emit("scalar", lambda e, ps=ps: e.activation(
                            sg_t[:], ps[:], AF.Silu), "S")
                        p.wait("gpsimd", "S", s_sg)
                        p.wait("gpsimd", "S", s_wb)
                        sgread = pmit(lambda e, m=m: e.tensor_tensor(
                            act[:, (m - 6) * 512:(m - 5) * 512],
                            act[:, (m - 6) * 512:(m - 5) * 512], sg_t[:], ALU.mult))
                        actw_done[m - 6] = pmit(lambda e, m=m: e.tensor_tensor(
                            act[:, (m - 6) * 512:(m - 5) * 512],
                            act[:, (m - 6) * 512:(m - 5) * 512], wb_e[:], ALU.mult))
                w1mm_done[e_i] = p.n["T"]
                yield
                p.wtok("tensor", w2_ld[e_i])
                p.wait("tensor", "P", actw_done[5])
                for i in range(8):
                    tg, dh = i // 2, i % 2
                    ps = psAB[i % 2]
                    if i >= 2:
                        p.wait("tensor", "S", w2acc[i - 2])
                    for k in range(6):
                        p.emit("tensor", lambda e, ps=ps, tg=tg, dh=dh, k=k: e.matmul(
                            ps[:], act[:, k * 512 + tg * 128: k * 512 + (tg + 1) * 128],
                            B[:, k * 1024 + dh * 512: k * 1024 + (dh + 1) * 512],
                            start=(k == 0), stop=(k == 5)), "T" if k == 5 else None)
                    mm = p.n["T"]
                    p.wait("scalar", "T", mm)
                    if e_i == 0:
                        if i == 0:
                            p.wait("scalar", "V", ffn_last_read)
                            p.wait("scalar", "P", ffn_pool_done)
                        w2acc[i] = p.emit("scalar", lambda e, ps=ps, tg=tg, dh=dh:
                                          e.activation(
                            ffn_acc[:, tg * D + dh * 512: tg * D + (dh + 1) * 512],
                            ps[:], AF.Copy), "S")
                    else:
                        sl2 = i % 2
                        p.wait("scalar", "P", w2tmp_free[sl2])
                        w2acc[i] = p.emit("scalar", lambda e, ps=ps, sl2=sl2:
                                          e.activation(
                            w2tmp[:, sl2 * 512:(sl2 + 1) * 512], ps[:], AF.Copy), "S")
                        p.wait("gpsimd", "S", w2acc[i])
                        w2tmp_free[sl2] = pmit(lambda e, tg=tg, dh=dh, sl2=sl2:
                                               e.tensor_add(
                            ffn_acc[:, tg * D + dh * 512: tg * D + (dh + 1) * 512],
                            ffn_acc[:, tg * D + dh * 512: tg * D + (dh + 1) * 512],
                            w2tmp[:, sl2 * 512:(sl2 + 1) * 512]))
                lastW2mm = p.n["T"]
                yield
                if e_i + 1 < NE:
                    p.wait("gpsimd", "T", lastW2mm)
                    t2 = {}
                    for k in range(6):
                        t2 = tokmerge(t2, g_dma(B[:, k * 1024:(k + 1) * 1024],
                                                w2t_h[e_i + 1, k * 128:(k + 1) * 128, :]))
                    w2_ld[e_i + 1] = t2

            ws2_ld = {}
            for k in range(6, 10):
                sl = k - 6
                p.wait("gpsimd", "T", ipw_free[sl])
                ws2_ld = tokmerge(ws2_ld,
                                  g_dma(A[:, IPW0 + sl * 1024: IPW0 + (sl + 1) * 1024],
                                        ws2t_h[k * 128:(k + 1) * 128, :]))
            p.wait("gpsimd", "V", ygz_done)
            p.wait("gpsimd", "S", p8_scalar_end)
            p.wait("gpsimd", "T", lastW2mm)
            ws2_ld = tokmerge(ws2_ld, g_dma(xn_tm[:, 0:1024],
                                            ws2t_h[10 * 128:11 * 128, :]))
            ws2_ld = tokmerge(ws2_ld, g_dma(h2b[:, 0:1024],
                                            ws2t_h[11 * 128:12 * 128, :]))
            for k in range(6):
                ws2_ld = tokmerge(ws2_ld, g_dma(B[:, k * 1024:(k + 1) * 1024],
                                                ws2t_h[k * 128:(k + 1) * 128, :]))

            shev = [0] * 24
            for m in range(24):
                ps = psAB[m % 2]
                sl = ws1_slot[m]
                if m % 8 == 7:
                    yield
                if m >= 2:
                    if m - 2 < 12:
                        p.wait("tensor", "S", shev[m - 2])
                    else:
                        p.wait("tensor", "P", shev[m - 2])
                else:
                    p.wait("tensor", "S", w2acc[6 + m])
                p.wtok("tensor", ws1_tok[m])
                for k in range(KD):
                    p.emit("tensor", lambda e, ps=ps, sl=sl, m=m, k=k, xn_fm=xn_fm:
                           e.matmul(
                        ps[:], A[:, W1R0 + sl * 1024 + k * 128:
                                  W1R0 + sl * 1024 + (k + 1) * 128],
                        xn_fm[:, k * 512:(k + 1) * 512],
                        start=(k == 0), stop=(k == KD - 1)), "T" if k == KD - 1 else None)
                mm = p.n["T"]
                ring_free[sl] = mm
                ring_gather()
                p.wait("scalar", "T", mm)
                if m < 12:
                    shev[m] = p.emit("scalar", lambda e, ps=ps, m=m: e.activation(
                        act[:, m * 512:(m + 1) * 512], ps[:], AF.Copy), "S")
                else:
                    p.wait("scalar", "P", sgread)
                    s_sg = p.emit("scalar", lambda e, ps=ps: e.activation(
                        sg_t[:], ps[:], AF.Silu), "S")
                    p.wait("gpsimd", "S", s_sg)
                    shev[m] = sgread = pmit(lambda e, m=m: e.tensor_tensor(
                        act[:, (m - 12) * 512:(m - 11) * 512],
                        act[:, (m - 12) * 512:(m - 11) * 512], sg_t[:], ALU.mult))
            xn_fm_free = p.n["T"]
            p.wtok("tensor", ws2_ld)
            p.wait("tensor", "P", shev[23])
            p.wait("tensor", "S", shev[11])
            wsacc = [0] * 8
            yield
            for i in range(8):
                tg, dh = i // 2, i % 2
                ps = psAB[i % 2]
                if i >= 2:
                    p.wait("tensor", "S", wsacc[i - 2])
                else:
                    p.wait("tensor", "S", w2acc[6 + i])
                for k in range(12):
                    if k < 6:
                        c0 = k * 1024 + dh * 512
                        rhs = B[:, c0: c0 + 512]
                    elif k < 10:
                        c0 = IPW0 + (k - 6) * 1024 + dh * 512
                        rhs = A[:, c0: c0 + 512]
                    elif k == 10:
                        rhs = xn_tm[:, dh * 512: dh * 512 + 512]
                    else:
                        rhs = h2b[:, dh * 512: dh * 512 + 512]
                    p.emit("tensor", lambda e, ps=ps, tg=tg, rhs=rhs, k=k: e.matmul(
                        ps[:], act[:, k * 512 + tg * 128: k * 512 + (tg + 1) * 128],
                        rhs, start=(k == 0), stop=(k == 11)), "T" if k == 11 else None)
                mm = p.n["T"]
                p.wait("scalar", "T", mm)
                sl2 = i % 2
                p.wait("scalar", "P", w2tmp_free[sl2])
                wsacc[i] = p.emit("scalar", lambda e, ps=ps, sl2=sl2: e.activation(
                    w2tmp[:, sl2 * 512:(sl2 + 1) * 512], ps[:], AF.Copy), "S")
                p.wait("gpsimd", "S", wsacc[i])
                w2tmp_free[sl2] = pmit(lambda e, tg=tg, dh=dh, sl2=sl2: e.tensor_add(
                    ffn_acc[:, tg * D + dh * 512: tg * D + (dh + 1) * 512],
                    ffn_acc[:, tg * D + dh * 512: tg * D + (dh + 1) * 512],
                    w2tmp[:, sl2 * 512:(sl2 + 1) * 512]))
            ffn_pool_done = p.n["P"]
            lastWs2mm = p.n["T"]
            for sl in range(4):
                ipw_free[sl] = lastWs2mm
            if not is_last:
                p.wait("gpsimd", "T", lastWs2mm)
                t2 = {}
                for k in range(6):
                    t2 = tokmerge(t2, g_dma(B[:, k * 1024:(k + 1) * 1024],
                                            w2t_h[0, k * 128:(k + 1) * 128, :]))
                live["w2_ld0"] = t2


        prev = None
        for c in range(NCH + 1):
            if prev is not None:
                rows0 = prev["rows0"]
                xn_fm = prev["xn_fm"]
                attn_add = [0] * NTG
                attn_add2 = [0] * 8
                op9 = [[0, 0] for _ in range(NTG)]
                wavebank = [psC, psD, psE, psF]
                p.wait("tensor", "S", p8_scalar_end)
                p.wait("tensor", "S", cd["free"])
                p.wait("tensor", "V", ygz_done)
                xl9 = [None] * NTG
                for wv in range(2):
                    gs = (0, 1) if wv == 0 else (2, 3)
                    for g in gs:
                        sl = g % 2
                        p.wait("sync", "V", xa_vfree[sl])
                        p.wait("sync", "S", xa_sfree[sl])
                        xl9[g] = s_dma(XAp[sl][:],
                                       x_h[rows0 + g * 128: rows0 + (g + 1) * 128, :])
                    if wv == 1:
                        p.wait("tensor", "V", attn_add[1])
                    opw_tok = [None] * NDT
                    opw_sl = [0] * NDT

                    def opw_gather(k, opw_tok=opw_tok, opw_sl=opw_sl):
                        sl = rctr2[0] % 4
                        rctr2[0] += 1
                        opw_sl[k] = sl
                        p.wait("gpsimd", "T", ring_free[sl])
                        opw_tok[k] = g_dma(
                            A[:, W1R0 + sl * 1024: W1R0 + (sl + 1) * 1024],
                            opw_h[k * 128:(k + 1) * 128, :])
                    for k in range(4):
                        opw_gather(k)
                    for k in range(NDT):
                        p.wtok("tensor", opw_tok[k])
                        sl = opw_sl[k]
                        for gi, g in enumerate(gs):
                            for half in range(2):
                                ps = wavebank[gi * 2 + half]
                                p.emit("tensor", lambda e, ps=ps, g=g, k=k, sl=sl,
                                       half=half: e.matmul(
                                    ps[:],
                                    szygz[:, k * 512 + g * 128: k * 512 + (g + 1) * 128],
                                    A[:, W1R0 + sl * 1024 + half * 512:
                                      W1R0 + sl * 1024 + half * 512 + 512],
                                    start=(k == 0), stop=False),
                                    "T" if gi == 1 and half == 1 else None)
                        ring_free[sl] = p.n["T"]
                        if k + 4 < NDT:
                            opw_gather(k + 4)
                    for gi, g in enumerate(gs):
                        for half in range(2):
                            ps = wavebank[gi * 2 + half]
                            op9[g][half] = p.emit(
                                "tensor", lambda e, ps=ps, half=half: e.matmul(
                                    ps[:], ones1[0:1, 0:128],
                                    opb_sb[0:1, half * 512: half * 512 + 512],
                                    start=False, stop=True), "T")
                    for gi, g in enumerate(gs):
                        sl = g % 2
                        xg = xsb[:, g * D:(g + 1) * D]
                        p.wait("scalar", "T", op9[g][0])
                        p.wtok("scalar", xl9[g])
                        s_xin = p.emit("scalar", lambda e, g=g, sl=sl: e.activation(
                            S1[:], XAp[sl][:], AF.Identity, bias=col(NEGMI + g),
                            scale=col(INV + g)), "S")
                        p.wait("vector", "S", s_xin)
                        for half in range(2):
                            ps = wavebank[gi * 2 + half]
                            p.wait("vector", "T", op9[g][half])
                            attn_add2[2 * g + half] = p.emit(
                                "vector", lambda e, ps=ps, half=half: e.tensor_add(
                                    S1[:, half * 512:(half + 1) * 512], ps[:],
                                    S1[:, half * 512:(half + 1) * 512]), "V")
                        attn_add[g] = p.n["V"]
                        p.emit("vector", lambda e: e.reduce_sum(col(C1), S1[:], axis=AX),
                               "V")
                        p.wait("scalar", "V", attn_add[g] + 1)
                        s_sq = p.emit("scalar", lambda e: e.activation(
                            xn_tm[:], S1[:], AF.Square, accum_out=col(C2)), "S")
                        p.wait("vector", "S", s_sq)
                        v_i2 = lnstats(C6, C7, C1, C2)
                        p.emit("vector", lambda e: e.tensor_scalar(
                            S1[:], S1[:], col(C6), col(C7), ALU.mult, ALU.add))
                        p.wtok("vector", out_tok[g] if c > 0 else {})
                        x2v = p.emit("vector", lambda e, xg=xg, sl=sl: e.tensor_add(
                            xg, XAp[sl][:], S1[:]), "V")
                        xa_vfree[sl] = x2v
                        xa_sfree[sl] = s_xin
                        p.emit("vector", lambda e, xg=xg: e.reduce_sum(col(C1), xg,
                                                                       axis=AX), "V")
                        p.wait("scalar", "V", x2v + 1)
                        s_sq3 = p.emit("scalar", lambda e, xg=xg: e.activation(
                            xn_tm[:], xg, AF.Square, accum_out=col(C2)), "S")
                        p.wait("vector", "S", s_sq3)
                        v_i3 = lnstats(C6, C7, C1, C2)
                        p.wait("scalar", "V", v_i3)
                        s_xn3 = p.emit("scalar", lambda e, xg=xg: e.activation(
                            xn_tm[:], xg, AF.Identity, bias=col(C7), scale=col(C6)), "S")
                        ps = psG
                        p.wait("tensor", "S", s_xn3)
                        p.wait("tensor", "S", evacT[0])
                        for d in range(KD):
                            p.emit("tensor", lambda e, ps=ps, d=d: e.transpose(
                                ps[:, d * 128:(d + 1) * 128],
                                xn_tm[:, d * 128:(d + 1) * 128],
                                id_bf[:]), "T" if d == KD - 1 else None)
                        t_tr3 = p.n["T"]
                        p.wait("scalar", "T", t_tr3)
                        p.emit("scalar", lambda e, ps=ps, g=g, xn_fm=xn_fm: e.activation(
                            bass.AP(xn_fm, g * 128, [[4096, 128], [512, 8], [1, 128]]),
                            ps[:, 0:1024], AF.Copy), "S")
                        evacT[0] = p.n["S"]
                outproj_done = op9[NTG - 1][1]
                xn3_ready = p.n["S"]

                p.wait("tensor", "S", xn3_ready)
                p.wait("tensor", "S", p8_scalar_end)
                p.wait("tensor", "V", ygz_done)
                gate_ps = [psE, psF, psH, psA]
                for g in range(NTG):
                    for k in range(KD):
                        p.emit("tensor", lambda e, g=g, k=k, xn_fm=xn_fm: e.matmul(
                            gate_ps[g][:, 0:NE],
                            xn_fm[:, k * 512 + g * 128: k * 512 + (g + 1) * 128],
                            gw_sb[:, k * NE:(k + 1) * NE],
                            start=(k == 0), stop=(k == KD - 1)),
                            "T" if k == KD - 1 else None)
                    t_gg = p.n["T"]
                    p.wait("scalar", "T", t_gg)
                    p.emit("scalar", lambda e, g=g: e.activation(
                        gsc[:, g * NE:(g + 1) * NE], gate_ps[g][:, 0:NE], AF.Copy), "S")
                s_gev = p.n["S"]
                p.wait("vector", "S", s_gev)
                t_g = p.n["T"]
                gsc3 = bass.AP(gsc, 0, [[32, 128], [8, 4], [1, 8]])
                gmask3 = bass.AP(gmask, 0, [[32, 128], [8, 4], [1, 8]])
                m1_bc = bass.AP(st, 12, [[24, 128], [1, 4], [0, 8]])
                m2_bc = bass.AP(st, 8, [[24, 128], [1, 4], [0, 8]])
                p.wait("vector", "T", t_g)
                p.emit("vector", lambda e: e.reduce_max(st[:, 12:16], gsc3, axis=AX))
                p.emit("vector", lambda e: e.tensor_tensor(gsc3, gsc3, m1_bc, ALU.subtract),
                       "V")
                v_sub = p.n["V"]
                p.wait("scalar", "V", v_sub)
                s_e = p.emit("scalar", lambda e: e.activation(gsc[:], gsc[:], AF.Exp), "S")
                p.wait("vector", "S", s_e)
                p.emit("vector", lambda e: e.reduce_sum(st[:, 8:12], gsc3, axis=AX))
                p.emit("vector", lambda e: e.reciprocal(st[:, 8:12], st[:, 8:12]))
                p.emit("vector", lambda e: e.tensor_tensor(gsc3, gsc3, m2_bc, ALU.mult))
                p.emit("vector", lambda e: e.reduce_max(st[:, 12:16], gsc3, axis=AX))
                p.emit("vector", lambda e: e.tensor_tensor(gmask3, gsc3, m1_bc, ALU.is_ge))
                p.emit("vector", lambda e: e.scalar_tensor_tensor(
                    gmask[:], gmask[:], -1e30, gsc[:], ALU.mult, ALU.add))
                p.emit("vector", lambda e: e.reduce_max(st[:, 8:12], gmask3, axis=AX))
                p.emit("vector", lambda e: e.tensor_tensor(gmask3, gsc3, m2_bc, ALU.is_ge))
                gdone_all = p.emit("vector", lambda e: e.tensor_tensor(
                    w_tm[:, 0:32], gsc[:], gmask[:], ALU.mult), "V")
                p.wait("tensor", "V", gdone_all)
                p.wait("tensor", "S", evacT[0])
                for g in range(NTG):
                    p.emit("tensor", lambda e, g=g: e.transpose(
                        psG[0:NE, g * 128:(g + 1) * 128], w_tm[:, g * NE:(g + 1) * NE],
                        id_bf[:]), "T" if g == NTG - 1 else None)
                t_wt = p.n["T"]
                p.wait("scalar", "T", t_wt)
                wfev_all = p.emit("scalar", lambda e: e.activation(
                    wf[:], psG[0:NE, 0:512], AF.Copy), "S")
                evacT[0] = p.n["S"]

                ffn_it = make_ffn(prev["xn_fm"], prev["w2_ld"], wfev_all,
                                  prev["p8s"], prev["ygz"], prev["ring_gather"],
                                  prev["w1_tok"], prev["ws1_tok"], prev["ws1_slot"],
                                  prev["ring_slotof"], c, c == NCH)
            else:
                ffn_it = iter(())
            if c < NCH:
                rows0 = c * TC
                xn_fm = xn_fmA if c % 2 == 0 else xn_fmB
                def ipw_gather(m):
                    sl = m % 4
                    p.wait("gpsimd", "T", ipw_free[sl])
                    ipw_tok[m] = p.dma("gpsimd",
                                       bass.AP(A, IPW0 + sl * 1024,
                                               [[24576, 128], [128, 8], [1, 128]]),
                                       bass.AP(ipw_h.tensor, m * 128,
                                               [[4096, 128], [524288, 8], [1, 128]]))
                for m in range(4):
                    ipw_gather(m)

                xload = [None] * NTG
                for g in (0, 1):
                    sl = g % 2
                    p.wait("sync", "V", xa_vfree[sl])
                    p.wait("sync", "S", xa_sfree[sl])
                    xload[g] = s_dma(XAp[sl][:],
                                     x_h[rows0 + g * 128: rows0 + (g + 1) * 128, :])
                for g in range(NTG):
                    xg = XAp[g % 2][:]
                    p.wtok("vector", xload[g])
                    p.emit("vector", lambda e, xg=xg: e.reduce_sum(col(C1), xg, axis=AX), "V")
                    p.wtok("scalar", xload[g])
                    s_sq = p.emit("scalar", lambda e, xg=xg: e.activation(
                        S1[:], xg, AF.Square, accum_out=col(C2)), "S")
                    p.wait("vector", "S", s_sq)
                    v_i = lnstats(INV + g, NEGMI + g, C1, C2)
                    p.wait("scalar", "V", v_i)
                    s_xin1 = p.emit("scalar", lambda e, xg=xg, g=g: e.activation(
                        S1[:], xg, AF.Identity, bias=col(NEGMI + g), scale=col(INV + g)), "S")
                    xa_vfree[g % 2] = p.n["V"]
                    xa_sfree[g % 2] = s_xin1
                    if g + 2 < NTG:
                        sl2 = (g + 2) % 2
                        p.wait("sync", "V", xa_vfree[sl2])
                        p.wait("sync", "S", xa_sfree[sl2])
                        xload[g + 2] = s_dma(
                            XAp[sl2][:],
                            x_h[rows0 + (g + 2) * 128: rows0 + (g + 3) * 128, :])
                    s_rms = p.emit("scalar", lambda e: e.activation(
                        xn_tm[:], S1[:], AF.Square, accum_out=col(C6)), "S")
                    p.wait("vector", "S", s_rms)
                    v_r = p.emit("vector", lambda e: e.tensor_scalar(
                        col(C7), col(C6), 1.0 / D, 1e-6, ALU.mult, ALU.add), "V")
                    p.wait("scalar", "V", v_r)
                    p.emit("scalar", lambda e: e.activation(col(C7), col(C7), AF.Ln))
                    p.emit("scalar", lambda e: e.activation(col(C7), col(C7), AF.Exp,
                                                            scale=col(KNHALF)))
                    s_xn = p.emit("scalar", lambda e: e.activation(
                        xn_tm[:], S1[:], AF.Copy, scale=col(C7)), "S")
                    ps = psG
                    p.wait("tensor", "S", s_xn)
                    p.wait("tensor", "S", evacT[0])
                    for d in range(KD):
                        p.emit("tensor", lambda e, ps=ps, d=d: e.transpose(
                            ps[:, d * 128:(d + 1) * 128], xn_tm[:, d * 128:(d + 1) * 128],
                            id_bf[:]), "T" if d == KD - 1 else None)
                    t_tr = p.n["T"]
                    p.wait("scalar", "T", t_tr)
                    if g == 0:
                        p.wait("scalar", "T", xn_fm_free)
                    for d in range(KD):
                        p.emit("scalar", lambda e, ps=ps, d=d, g=g, xn_fm=xn_fm: e.activation(
                            xn_fm[:, d * 512 + g * 128: d * 512 + (g + 1) * 128],
                            ps[:, d * 128:(d + 1) * 128], AF.Copy, scale=ns_sb[:, d:d + 1]), "S")
                    evacT[0] = p.n["S"]
                xn1_ready = p.n["S"]

                p.wait("tensor", "S", xn1_ready)
                p.wait("tensor", "S", ainit)
                evac3 = [0] * 32
                p.wait("tensor", "V", attn_add[NTG - 1] if c > 0 else 0)
                p.wait("tensor", psab_last[0][0], psab_last[0][1])
                p.wait("tensor", psab_last[1][0], psab_last[1][1])
                live["evac3"] = evac3
                for m in range(32):
                    ps = psABCD[m % 4]
                    sl = m % 4
                    if m >= 4:
                        p.wait("tensor", "S", evac3[m - 4])
                    p.wtok("tensor", ipw_tok[m])
                    for k in range(KD):
                        p.emit("tensor", lambda e, ps=ps, sl=sl, m=m, k=k, xn_fm=xn_fm:
                               e.matmul(
                            ps[:], A[:, IPW0 + sl * 1024 + k * 128: IPW0 + sl * 1024 + (k + 1) * 128],
                            xn_fm[:, k * 512:(k + 1) * 512],
                            start=(k == 0), stop=(k == KD - 1)), "T" if k == KD - 1 else None)
                    mm = p.n["T"]
                    ipw_free[sl] = mm
                    if m + 4 < 32:
                        ipw_gather(m + 4)
                    p.wait("scalar", "T", mm)
                    if m < NDT:
                        p.wait("scalar", "V", uread[m])
                        p.emit("scalar", lambda e, ps=ps, m=m: e.activation(
                            u_pre[:, m * 516 + 4: m * 516 + 516], ps[:], AF.Identity,
                            bias=ipb_sb[:, m:m + 1]))
                    else:
                        p.wait("scalar", "V", ygz_done)
                        p.wait("scalar", "T", outproj_done)
                        p.emit("scalar", lambda e, ps=ps, m=m: e.activation(
                            szygz[:, (m - NDT) * 512: (m - NDT + 1) * 512], ps[:], AF.Silu,
                            bias=ipb_sb[:, m:m + 1]))
                    evac3[m] = p.n["S"]

                if c == 0:
                    t2 = {}
                    for k in range(6):
                        t2 = tokmerge(t2, g_dma(B[:, k * 1024:(k + 1) * 1024],
                                                w2t_h[0, k * 128:(k + 1) * 128, :]))
                    live["w2_ld0"] = t2
                p.wait("gpsimd", "T", ipw_free[0])
                dtow_ld = g_dma(A[:, IPW0: IPW0 + 1024], dtow_h[:, :])
                w2_ld = [None] * NE
                w2_ld[0] = live["w2_ld0"]
                ring_slotof = {}
                w1_tok = [[None] * 12 for _ in range(NE)]
                ws1_tok = [None] * 24
                ws1_slot = [0] * 24
                gq = [("w1", e, m) for e in range(NE) for m in range(12)] + \
                     [("ws1", 0, m) for m in range(24)]
                gq_next = [0]

                def ring_gather(gq=gq, gq_next=gq_next, w1_tok=w1_tok,
                                ws1_tok=ws1_tok, ws1_slot=ws1_slot,
                                ring_slotof=ring_slotof):
                    if gq_next[0] >= len(gq):
                        return
                    kind, e, m = gq[gq_next[0]]
                    gq_next[0] += 1
                    sl = rctr2[0] % 4
                    rctr2[0] += 1
                    p.wait("gpsimd", "T", ring_free[sl])
                    if kind == "w1":
                        tok = p.dma("gpsimd",
                                    bass.AP(A, W1R0 + sl * 1024,
                                            [[24576, 128], [128, 8], [1, 128]]),
                                    bass.AP(w1t_h.tensor, e * 1572864 + m * 128,
                                            [[1536, 128], [196608, 8], [1, 128]]))
                        w1_tok[e][m] = tok
                        ring_slotof[(e, m)] = sl
                    else:
                        tok = p.dma("gpsimd",
                                    bass.AP(A, W1R0 + sl * 1024,
                                            [[24576, 128], [128, 8], [1, 128]]),
                                    bass.AP(ws1t_h.tensor, m * 128,
                                            [[3072, 128], [393216, 8], [1, 128]]))
                        ws1_tok[m] = tok
                        ws1_slot[m] = sl

                usil = [0] * NDT
                for d in range(NDT):
                    ub = d * 516
                    p.wait("vector", "S", evac3[d])
                    p.emit("vector", lambda e, d=d, ub=ub: e.tensor_copy(
                        hst[:, d * 3:(d + 1) * 3], u_pre[:, ub + 513: ub + 516]))
                    p.emit("vector", lambda e, ub=ub, d=d: e.tensor_scalar(
                        yac2[:, 0:512], u_pre[:, ub + 1: ub + 513], ck_sb[:, d * 4: d * 4 + 1],
                        None, ALU.mult))
                    v_cv = None
                    for j in (1, 2, 3):
                        v_cv = p.emit("vector", lambda e, ub=ub, d=d, j=j: e.scalar_tensor_tensor(
                            yac2[:, 0:512], u_pre[:, ub + 1 + j: ub + 513 + j],
                            ck_sb[:, d * 4 + j: d * 4 + j + 1], yac2[:, 0:512],
                            ALU.mult, ALU.add), "V")
                    p.wait("scalar", "V", v_cv)
                    usil[d] = p.emit("scalar", lambda e, ub=ub, d=d: e.activation(
                        u_pre[:, ub + 4: ub + 516], yac2[:, 0:512], AF.Silu,
                        bias=cb_sb[:, d:d + 1]), "S")
                    p.wait("vector", "S", usil[d])
                    p.emit("vector", lambda e, ub=ub, d=d: e.tensor_copy(
                        u_pre[:, ub + 1: ub + 4], hst[:, d * 3:(d + 1) * 3]), "V")
                conv_done_v = p.n["V"]

                p.wait("tensor", "S", usil[NDT - 1])
                p.wait("tensor", "S", evac3[28])
                for half in range(2):
                    for k in range(NDT):
                        p.emit("tensor", lambda e, k=k, half=half: e.matmul(
                            psA[half * DTR:(half + 1) * DTR, :],
                            dtiw_sb[:, k * DTR:(k + 1) * DTR],
                            u_pre[:, k * 516 + 4: k * 516 + 516],
                            start=(k == 0), stop=(k == NDT - 1)),
                            "T" if k == NDT - 1 else None)
                t_dtr = p.n["T"]
                p.wait("scalar", "T", t_dtr)
                s_dtr = p.emit("scalar", lambda e: e.activation(
                    dtr_sb[:], psA[:], AF.Identity, bias=dtib_sb[:, 0:1]), "S")

                p.wait("tensor", "S", evac3[31])
                bc_ev = []
                for (w_sb, bias_sb, dst, psx) in ((bpw_sb, bpb_sb, b_sb, psE),
                                                  (cpw_sb, cpb_sb, c_sb, psF)):
                    for k in range(NDT):
                        p.emit("tensor", lambda e, w_sb=w_sb, k=k, psx=psx: e.matmul(
                            psx[0:DS, :], w_sb[:, k * DS:(k + 1) * DS],
                            u_pre[:, k * 516 + 4: k * 516 + 516],
                            start=(k == 0), stop=(k == NDT - 1)), "T" if k == NDT - 1 else None)
                    t_bc = p.n["T"]
                    p.wait("scalar", "T", t_bc)
                    bc_ev.append(p.emit("scalar", lambda e, dst=dst, bias_sb=bias_sb, psx=psx:
                                        e.activation(dst[:], psx[0:DS, :], AF.Tanh,
                                                     bias=bias_sb[:, 0:1]), "S"))
                s_bc = p.n["S"]

                live["s_dtr"] = s_dtr
                sp_ev = [0] * NDT
                last_vclip = 0
                cd["free"] = max(cd["free"], s_dtr)
                p.wait("scalar", "V", ffn_last_read)
                p.wait("scalar", "T", lastWs2mm)
                p.wait("tensor", "S", s_bc)
                p.wait("tensor", "S", cd["free"])
                p.wait("tensor", "V", attn_add[NTG - 1] if c > 0 else 0)
                p.wait("scalar", "V", live["bball_rd"])
                p.wait("scalar", "P", live["bball_pool"])
                fill_ev = [0] * DS
                fbanks = [(psC, psD), (psE, psF)]
                for s in range(DS):
                    pb, pc = fbanks[s % 2]
                    if s >= 2:
                        p.wait("tensor", "S", fill_ev[s - 2])
                    p.emit("tensor", lambda e, s=s, pb=pb: e.matmul(
                        pb[:], sel_sb[:, s * 128:(s + 1) * 128], b_sb[:],
                        start=True, stop=True))
                    p.emit("tensor", lambda e, s=s, pc=pc: e.matmul(
                        pc[:], sel_sb[:, s * 128:(s + 1) * 128], c_sb[:],
                        start=True, stop=True), "T")
                    t_f = p.n["T"]
                    p.wait("scalar", "T", t_f)
                    p.emit("scalar", lambda e, s=s, pb=pb: e.activation(
                        A[:, BB0 + s * 512: BB0 + (s + 1) * 512], pb[:], AF.Copy))
                    fill_ev[s] = p.emit("scalar", lambda e, s=s, pc=pc: e.activation(
                        A[:, CB0 + s * 512: CB0 + (s + 1) * 512], pc[:], AF.Copy), "S")
                cd["free"] = fill_ev[DS - 1]
                scan_hist = [0, 0, 0, 0]
                da_ps4 = [psF, psH, psE, psD]
                da_ev = [0, 0, 0, 0]
                h2p = [h2, h2b]
                yadd_p = [0, 0]
                yac2_free = 0
                for d in range(NDT):
                    ub = d * 516
                    p.wait("tensor", "S", s_dtr)
                    if d > 0:
                        p.wait("tensor", "S", da_ev[2])
                        p.wait("tensor", "V", scan_hist[2])
                    else:
                        p.wait("tensor", "S", cd["free"])
                    if d == 0:
                        p.wtok("tensor", dtow_ld)
                    p.emit("tensor", lambda e, d=d: e.matmul(
                        psE[:], A[(d // 8) * 64:(d // 8) * 64 + 64,
                                  IPW0 + (d % 8) * 128: IPW0 + (d % 8 + 1) * 128],
                        dtr_sb[(d // 8) * 64:(d // 8) * 64 + 64, :],
                        start=True, stop=True), "T")
                    t_dt = p.n["T"]
                    p.wait("scalar", "T", t_dt)
                    p.wait("scalar", "V", conv_done_v if d == 0 else last_vclip)
                    p.emit("scalar", lambda e, d=d: e.activation(
                        dt_t[:], psE[:], AF.Exp, bias=dtob_sb[:, d:d + 1]))
                    sp_ev[d] = p.emit("scalar", lambda e: e.activation(
                        dt_t[:], dt_t[:], AF.Ln, bias=col(KONE)), "S")
                    p.wait("vector", "S", sp_ev[d])
                    p.emit("vector", lambda e: e.tensor_scalar(dt_t[:], dt_t[:], 0.001, 0.1,
                                                               ALU.max, ALU.min))
                    v_clip = p.emit("vector", lambda e, ub=ub: e.tensor_tensor(
                        w_t[:], dt_t[:], u_pre[:, ub + 4: ub + 516], ALU.mult), "V")
                    uread[d] = v_clip
                    last_vclip = v_clip
                    w_rep = bass.AP(w_t, 0, [[512, 128], [0, 2], [1, 512]])
                    h2_pool_rd = [0, 0]
                    for grp in range(8):
                        s0 = 2 * grp
                        hcur = h2p[grp % 2]
                        if h2_pool_rd[grp % 2]:
                            p.wait("vector", "P", h2_pool_rd[grp % 2])
                        for j in (0, 1):
                            sI = s0 + j
                            bk = sI % 4
                            p.wait("scalar", "V", max(v_clip, scan_hist[bk]))
                            da_ev[bk] = p.emit("scalar", lambda e, d=d, sI=sI, bk=bk:
                                               e.activation(
                                da_ps4[bk][:], dt_t[:], AF.Exp,
                                scale=a_sb[:, d * DS + sI: d * DS + sI + 1]), "S")
                        s_da = p.n["S"]
                        if d == 0:
                            p.wait("vector", "S", fill_ev[s0 + 1])
                        p.emit("vector", lambda e, w_rep=w_rep, s0=s0: e.tensor_tensor(
                            db2[:], w_rep,
                            A[:, BB0 + s0 * 512: BB0 + (s0 + 2) * 512], ALU.mult), "V")
                        p.wait("vector", "S", s_da)
                        for j in (0, 1):
                            sI = s0 + j
                            bk = sI % 4
                            scan_hist[bk] = p.emit(
                                "vector", lambda e, d=d, sI=sI, j=j, bk=bk,
                                hcur=hcur: e.tensor_tensor_scan(
                                    hcur[:, j * 512:(j + 1) * 512], da_ps4[bk][:],
                                    db2[:, j * 512:(j + 1) * 512],
                                    carry[:, d * DS + sI: d * DS + sI + 1],
                                    ALU.mult, ALU.add), "V")
                        tts_done = scan_hist[(s0 + 1) % 4]
                        p.emit("vector", lambda e, d=d, s0=s0, hcur=hcur: e.tensor_copy(
                            carry[:, d * DS + s0: d * DS + s0 + 2],
                            bass.AP(hcur, 511, [[1024, 128], [512, 2]])))
                        if grp == 0:
                            p.wait("gpsimd", "V", max(tts_done, yac2_free))
                            yadd_last = pmit(lambda e, s0=s0, hcur=hcur: e.tensor_tensor(
                                yac2[:], hcur[:],
                                A[:, CB0 + s0 * 512: CB0 + (s0 + 2) * 512],
                                ALU.mult))
                            h2_pool_rd[0] = yadd_last
                        else:
                            p.wait("vector", "P", yadd_p[grp % 2])
                            v_c = p.emit("vector", lambda e, s0=s0, grp=grp,
                                         hcur=hcur: e.tensor_tensor(
                                db3[:, (grp % 2) * 1024:(grp % 2 + 1) * 1024],
                                hcur[:],
                                A[:, CB0 + s0 * 512: CB0 + (s0 + 2) * 512],
                                ALU.mult), "V")
                            p.wait("gpsimd", "V", v_c)
                            yadd_last = yadd_p[grp % 2] = pmit(
                                lambda e, grp=grp: e.tensor_add(
                                    yac2[:], yac2[:],
                                    db3[:, (grp % 2) * 1024:(grp % 2 + 1) * 1024]))
                    p.wait("vector", "P", yadd_last)
                    p.emit("vector", lambda e: e.tensor_add(
                        yac2[:, 0:512], yac2[:, 0:512], yac2[:, 512:1024]))
                    p.emit("vector", lambda e, ub=ub, d=d: e.scalar_tensor_tensor(
                        yac2[:, 0:512], u_pre[:, ub + 4: ub + 516], dd_sb[:, d:d + 1],
                        yac2[:, 0:512], ALU.mult, ALU.add))
                    yac2_free = p.emit("vector", lambda e, d=d: e.tensor_tensor(
                        szygz[:, d * 512:(d + 1) * 512], yac2[:, 0:512],
                        szygz[:, d * 512:(d + 1) * 512], ALU.mult), "V")
                    next(ffn_it, None)
                ygz_done = p.n["V"]
                live["bball_rd"] = ygz_done
                live["bball_pool"] = p.n["P"]
                p8_scalar_end = p.n["S"]
                ipw_free[0] = p.n["T"]

            for _ in ffn_it:
                pass
            xn_fm_free = p.n["T"]
            if prev is not None:
                rows0 = prev["rows0"]
                p.wait("vector", "P", ffn_pool_done)
                for g in range(NTG):
                    xg = xsb[:, g * D:(g + 1) * D]
                    fg = ffn_acc[:, g * D:(g + 1) * D]
                    p.emit("vector", lambda e, fg=fg: e.reduce_sum(col(C1), fg, axis=AX), "V")
                    v_s1 = p.n["V"]
                    p.wait("scalar", "V", v_s1)
                    s_sq = p.emit("scalar", lambda e, fg=fg: e.activation(
                        S1[:], fg, AF.Square, accum_out=col(C2)), "S")
                    p.wait("vector", "S", s_sq)
                    v_i4 = lnstats(C6, C7, C1, C2)
                    p.wait("scalar", "V", v_i4)
                    s_ln4 = p.emit("scalar", lambda e, fg=fg: e.activation(
                        S1[:], fg, AF.Identity, bias=col(C7), scale=col(C6)), "S")
                    p.wait("vector", "S", s_ln4)
                    x3v = p.emit("vector", lambda e, xg=xg: e.tensor_add(xg, xg, S1[:]), "V")
                    p.wait("sync", "V", x3v)
                    out_tok[g] = s_dma(out_h[rows0 + g * 128: rows0 + (g + 1) * 128, :], xg)
                ffn_last_read = p.n["V"]
            if c < NCH:
                prev = dict(rows0=c * TC, xn_fm=xn_fm, w2_ld=w2_ld,
                            p8s=p8_scalar_end, ygz=ygz_done,
                            ring_gather=ring_gather, w1_tok=w1_tok,
                            ws1_tok=ws1_tok, ws1_slot=ws1_slot,
                            ring_slotof=ring_slotof)
            else:
                prev = None


        final_sync = {lane: p.n[lane] for lane in XLANES}

        semmap = {k: sem[k] for k in SEMKEYS}
        block = ec(nc.Block())

        def play(engname, eng):
            selfkey = {"vector": "V", "scalar": "S"}.get(engname)
            cnt = 0
            for op in p.ops[engname]:
                if op[0] == "wait":
                    _, key, val = op
                    if key == selfkey and val >= cnt:
                        continue
                    eng.wait_ge(semmap[key], val)
                else:
                    fn, inckey = op
                    if selfkey is not None and cnt > 0:
                        eng.wait_ge(semmap[selfkey], cnt)
                    ins = fn(eng)
                    if inckey is not None:
                        ins.then_inc(semmap[inckey], 16 if len(inckey) > 1 else 1)
                        cnt += 1

        @block.tensor
        def _(e):
            play("tensor", e)

        @block.vector
        def _(e):
            play("vector", e)

        @block.scalar
        def _(e):
            play("scalar", e)

        @block.sync
        def _(e):
            play("sync", e)
            for lane, val in final_sync.items():
                if val:
                    e.wait_ge(sem[lane], val)

        @block.gpsimd
        def _(e):
            play("gpsimd", e)

    return nc


_NC_CACHE = None


def kernel(**inputs):
    global _NC_CACHE
    ii = {k: np.ascontiguousarray(np.asarray(v, np.float32)) for k, v in inputs.items()}

    def fm(v, ncols):
        v = v.reshape(-1, 128, ncols)
        return np.ascontiguousarray(v.transpose(1, 0, 2).reshape(128, -1))

    shared = {
        "ipw": ii["L0_attn_in_proj_w"],
        "ipb": np.ascontiguousarray(ii["L0_attn_in_proj_b"].reshape(32, 128).T),
        "opw": ii["L0_attn_out_proj_w"],
        "opbr": ii["L0_attn_out_proj_b"].reshape(1, D),
        "dtiw": fm(ii["L0_attn_dt_in_w"], DTR),
        "dtib": np.tile(ii["L0_attn_dt_in_b"].reshape(DTR, 1), (2, 1)),
        "dtow": np.concatenate([ii["L0_attn_dt_out_w"][:, 0:1024],
                                ii["L0_attn_dt_out_w"][:, 1024:2048]], axis=0),
        "dtob": np.ascontiguousarray(ii["L0_attn_dt_out_b"].reshape(NDT, 128).T),
        "dtbb": np.ascontiguousarray(ii["L0_attn_dt_bias"].reshape(NDT, 128).T),
        "bpw": fm(ii["L0_attn_b_proj_w"], DS),
        "bpb": ii["L0_attn_b_proj_b"].reshape(DS, 1),
        "cpw": fm(ii["L0_attn_c_proj_w"], DS),
        "cpb": ii["L0_attn_c_proj_b"].reshape(DS, 1),
        "ck": fm(np.ascontiguousarray(ii["L0_attn_conv_kernel"].T), 4),
        "cb": np.ascontiguousarray(ii["L0_attn_conv_bias"].reshape(NDT, 128).T),
        "alog": fm(ii["L0_attn_a_log"], DS),
        "dd": np.ascontiguousarray(ii["L0_attn_d"].reshape(NDT, 128).T),
        "ns": np.ascontiguousarray(ii["L0_attn_norm_scale"].reshape(KD, 128).T),
        "gw": fm(np.ascontiguousarray(ii["L0_ffn_gate_w"].T), NE),
        "w1t": np.ascontiguousarray(ii["L0_ffn_W1"].transpose(0, 2, 1)),
        "w2t": np.ascontiguousarray(ii["L0_ffn_W2"].transpose(0, 2, 1)),
        "ws1t": np.ascontiguousarray(ii["L0_ffn_Ws1"].T),
        "ws2t": np.ascontiguousarray(ii["L0_ffn_Ws2"].T),
        "ident": np.eye(128, dtype=np.float32),
        "sel": np.kron(np.eye(DS, dtype=np.float32), np.ones((1, 128), np.float32)),
    }
    x = ii["L0_attn_x"]
    in_maps = []
    for core in range(8):
        m = dict(shared)
        m["x"] = np.ascontiguousarray(x[core])
        in_maps.append(m)

    if _NC_CACHE is None:
        _NC_CACHE = build_core()
    res = run_bass_kernel_spmd(_NC_CACHE, in_maps, core_ids=list(range(8)))
    out = np.stack([np.asarray(r["out"], np.float32) for r in res.results], axis=0)
    return out


# version 11 (final)
# speedup vs baseline: 1.0573x; 1.0573x over previous
import contextlib
import numpy as np
import concourse.bass as bass
import concourse.mybir as mybir
from concourse.bass_utils import run_bass_kernel_spmd

F32 = mybir.dt.float32
BF16 = mybir.dt.bfloat16
AF = mybir.ActivationFunctionType
ALU = mybir.AluOpType
AX = mybir.AxisListType.X

T, D, INNER, DS, DTR = 4096, 1024, 2048, 16, 64
TC = 512
NCH = T // TC
NTG = TC // 128
NDT = INNER // 128
KD = D // 128
NE = 8



GLANES = ["G%d" % i for i in range(8)]
XLANES = ["X%d" % i for i in range(4)]
DLANES = {"gpsimd": GLANES, "sync": XLANES}
SEMKEYS = ["T", "V", "S", "P"] + GLANES + XLANES


class P:

    def __init__(self):
        self.ops = {k: [] for k in ("tensor", "vector", "scalar", "sync", "gpsimd")}
        self.n = {k: 0 for k in SEMKEYS}
        self.rr = {"gpsimd": 0, "sync": 0}

    def emit(self, eng, fn, inckey=None):
        if eng == "vector":
            inckey = "V"
        elif eng == "scalar":
            inckey = "S"
        self.ops[eng].append((fn, inckey))
        if inckey is not None:
            return self.inc(inckey)
        return None

    def wait(self, eng, sem_key, val):
        if val and val > 0:
            self.ops[eng].append(("wait", sem_key, val))

    def wtok(self, eng, tok):
        if tok:
            for lane, val in tok.items():
                self.wait(eng, lane, val)

    def dma(self, eng, dst, src):
        lanes = DLANES[eng]
        i = self.rr[eng]
        self.rr[eng] = (i + 1) % len(lanes)
        lane = lanes[i]
        self.wait(eng, lane, self.n[lane])
        self.ops[eng].append((lambda e, dst=dst, src=src: e.dma_start(dst, src), lane))
        self.n[lane] += 16
        return {lane: self.n[lane]}

    def inc(self, sem_key):
        self.n[sem_key] += 16 if len(sem_key) > 1 else 1
        return self.n[sem_key]


def tokmerge(*toks):
    d = {}
    for t in toks:
        for lane, v in t.items():
            d[lane] = max(d.get(lane, 0), v)
    return d


def build_core():
    nc = bass.Bass("TRN2", target_bir_lowering=False)

    def din(name, shape):
        return nc.dram_tensor(name, list(shape), F32, kind="ExternalInput").ap()

    x_h = din("x", (T, D))
    ipw_h = din("ipw", (D, 2 * INNER))
    ipb_h = din("ipb", (128, 32))
    opw_h = din("opw", (INNER, D))
    opbr_h = din("opbr", (1, D))
    dtiw_h = din("dtiw", (128, NDT * DTR))
    dtib_h = din("dtib", (128, 1))
    dtow_h = din("dtow", (128, 1024))
    dtob_h = din("dtob", (128, NDT))
    dtbb_h = din("dtbb", (128, NDT))
    bpw_h = din("bpw", (128, NDT * DS))
    bpb_h = din("bpb", (DS, 1))
    cpw_h = din("cpw", (128, NDT * DS))
    cpb_h = din("cpb", (DS, 1))
    ck_h = din("ck", (128, NDT * 4))
    cb_h = din("cb", (128, NDT))
    alog_h = din("alog", (128, NDT * DS))
    dd_h = din("dd", (128, NDT))
    ns_h = din("ns", (128, KD))
    gw_h = din("gw", (128, KD * NE))
    w1t_h = din("w1t", (NE, D, 1536))
    w2t_h = din("w2t", (NE, 768, D))
    ws1t_h = din("ws1t", (D, 3072))
    ws2t_h = din("ws2t", (1536, D))
    id_h = din("ident", (128, 128))
    sel_h = din("sel", (DS, DS * 128))
    out_h = nc.dram_tensor("out", [T, D], F32, kind="ExternalOutput").ap()

    p = P()
    with contextlib.ExitStack() as ctx:
        ec = ctx.enter_context
        sem = {k: ec(nc.semaphore("sem" + k)) for k in SEMKEYS}

        A = ec(nc.sbuf_tensor("A", [128, 24576], BF16))
        B = ec(nc.sbuf_tensor("B", [128, 6144], BF16))
        xsb = ec(nc.sbuf_tensor("xsb", [128, NTG * D], F32))
        XA1 = ec(nc.sbuf_tensor("XA1", [128, D], F32))
        u_pre = ec(nc.sbuf_tensor("u_pre", [128, NDT * 516], BF16))
        szygz = ec(nc.sbuf_tensor("szygz", [128, NDT * 512], BF16))
        xn_fmA = ec(nc.sbuf_tensor("xn_fmA", [128, KD * 512], BF16))
        xn_fmB = ec(nc.sbuf_tensor("xn_fmB", [128, KD * 512], BF16))
        xn_tm = ec(nc.sbuf_tensor("xn_tm", [128, D], BF16))
        S1 = ec(nc.sbuf_tensor("S1", [128, D], F32))
        st = ec(nc.sbuf_tensor("st", [128, 24], F32))
        carry = ec(nc.sbuf_tensor("carry", [128, NDT * DS], F32))
        hst = ec(nc.sbuf_tensor("hst", [128, NDT * 3], BF16))
        dt_t = ec(nc.sbuf_tensor("dt_t", [128, 512], F32))
        yac2 = ec(nc.sbuf_tensor("yac2", [128, 1024], F32))
        w_t = ec(nc.sbuf_tensor("w_t", [128, 512], BF16))
        db2 = ec(nc.sbuf_tensor("db2", [128, 1024], BF16))
        db3 = ec(nc.sbuf_tensor("db3", [128, 2048], BF16))
        h2 = ec(nc.sbuf_tensor("h2", [128, 1024], BF16))
        h2b = ec(nc.sbuf_tensor("h2b", [128, 1024], BF16))
        sg_t = ec(nc.sbuf_tensor("sg_t", [128, 512], BF16))
        wb_e = ec(nc.sbuf_tensor("wb_e", [128, 512], BF16))
        w2tmp = ec(nc.sbuf_tensor("w2tmp", [128, 1024], F32))
        dtr_sb = ec(nc.sbuf_tensor("dtr_sb", [128, 512], BF16))
        b_sb = ec(nc.sbuf_tensor("b_sb", [DS, 512], BF16))
        c_sb = ec(nc.sbuf_tensor("c_sb", [DS, 512], BF16))
        wf = ec(nc.sbuf_tensor("wf", [NE, 512], BF16))
        sel_sb = ec(nc.sbuf_tensor("sel_sb", [DS, DS * 128], BF16))
        w_tm = ec(nc.sbuf_tensor("w_tm", [128, NTG * NE], BF16))
        gsc = ec(nc.sbuf_tensor("gsc", [128, NTG * NE], F32))
        gmask = ec(nc.sbuf_tensor("gmask", [128, NTG * NE], F32))
        ones1 = ec(nc.sbuf_tensor("ones1", [1, 128], BF16))
        id_bf = ec(nc.sbuf_tensor("id_bf", [128, 128], BF16))
        opb_sb = ec(nc.sbuf_tensor("opb_sb", [1, D], BF16))
        ipb_sb = ec(nc.sbuf_tensor("ipb_sb", [128, 32], F32))
        dtiw_sb = ec(nc.sbuf_tensor("dtiw_sb", [128, NDT * DTR], BF16))
        dtib_sb = ec(nc.sbuf_tensor("dtib_sb", [128, 1], F32))
        dtob_sb = ec(nc.sbuf_tensor("dtob_sb", [128, NDT], F32))
        dtbb_sb = ec(nc.sbuf_tensor("dtbb_sb", [128, NDT], F32))
        bpw_sb = ec(nc.sbuf_tensor("bpw_sb", [128, NDT * DS], BF16))
        bpb_sb = ec(nc.sbuf_tensor("bpb_sb", [DS, 1], F32))
        cpw_sb = ec(nc.sbuf_tensor("cpw_sb", [128, NDT * DS], BF16))
        cpb_sb = ec(nc.sbuf_tensor("cpb_sb", [DS, 1], F32))
        ck_sb = ec(nc.sbuf_tensor("ck_sb", [128, NDT * 4], F32))
        cb_sb = ec(nc.sbuf_tensor("cb_sb", [128, NDT], F32))
        a_sb = ec(nc.sbuf_tensor("a_sb", [128, NDT * DS], F32))
        dd_sb = ec(nc.sbuf_tensor("dd_sb", [128, NDT], F32))
        ns_sb = ec(nc.sbuf_tensor("ns_sb", [128, KD], F32))
        gw_sb = ec(nc.sbuf_tensor("gw_sb", [128, KD * NE], BF16))
        resv = ec(nc.sbuf_tensor("resv", [128, 16384], BF16))
        raddr = nc.lookup_mloc(resv).addr
        ffn_acc = nc.alloc_sbuf_tensor_at("ffn_acc", [128, NTG * D], F32, offset=raddr)
        act = nc.alloc_sbuf_tensor_at("act", [128, 12 * 512], BF16, offset=raddr + 16384)
        XA0 = nc.alloc_sbuf_tensor_at("XA0", [128, D], F32, offset=raddr + 28672)
        XAp = [XA0, XA1]
        psA = ec(nc.psum_tensor("psA", [128, 512], F32))
        psB = ec(nc.psum_tensor("psB", [128, 512], F32))
        psC = ec(nc.psum_tensor("psC", [128, 512], F32))
        psD = ec(nc.psum_tensor("psD", [128, 512], F32))
        psE = ec(nc.psum_tensor("psE", [128, 512], F32))
        psF = ec(nc.psum_tensor("psF", [128, 512], F32))
        psG = ec(nc.psum_tensor("psG", [128, 1024], BF16))
        psH = ec(nc.psum_tensor("psH", [128, 512], F32))
        psAB = [psA, psB]
        psABCD = [psA, psB, psC, psD]
        psCDE = [psC, psD, psE]
        psEF = [psE, psF]
        psCD = [psC, psD]

        MEAN, INV, NEGMI = 0, 4, 8
        C1, C2, C3, C4, C5, C6, C7 = 12, 13, 14, 15, 16, 17, 18
        KEPS, KONE, KNHALF, KNEG1 = 19, 20, 21, 22

        def col(i):
            return st[:, i:i + 1]

        def g_dma(dst, src):
            return p.dma("gpsimd", dst, src)

        def pmit(fn):
            p.wait("gpsimd", "P", p.n["P"])
            return p.emit("gpsimd", fn, "P")

        def s_dma(dst, src):
            return p.dma("sync", dst, src)

        def lnstats(invcol, negcol, sumcol, sqcol):
            p.emit("vector", lambda e: e.tensor_scalar(col(C3), col(sumcol), 1.0 / D,
                                                       None, ALU.mult), "V")
            p.emit("vector", lambda e: e.tensor_tensor(col(C4), col(C3), col(C3),
                                                       ALU.mult), "V")
            p.emit("vector", lambda e: e.tensor_scalar(col(C5), col(sqcol), 1.0 / D,
                                                       None, ALU.mult), "V")
            v_var = p.emit("vector", lambda e: e.tensor_tensor(col(C5), col(C5), col(C4),
                                                               ALU.subtract), "V")
            p.wait("scalar", "V", v_var)
            p.emit("scalar", lambda e: e.activation(col(C5), col(C5), AF.Ln,
                                                    bias=col(KEPS)))
            s_sd = p.emit("scalar", lambda e, ic=invcol: e.activation(
                col(ic), col(C5), AF.Exp, scale=col(KNHALF)), "S")
            p.wait("vector", "S", s_sd)
            v_i = p.emit("vector", lambda e, ic=invcol, ng=negcol: e.scalar_tensor_tensor(
                col(ng), col(C3), -1.0, col(ic), ALU.mult, ALU.mult), "V")
            return v_i

        resb = {}
        for dst, src in ((dtiw_sb[:], dtiw_h),
                         (bpw_sb[:], bpw_h), (cpw_sb[:], cpw_h),
                         (gw_sb[:], gw_h), (id_bf[:], id_h), (opb_sb[:], opbr_h),
                         (sel_sb[:], sel_h)):
            resb = tokmerge(resb, g_dma(dst, src))
        resx = {}
        for dst, src in ((ipb_sb[:], ipb_h), (dtib_sb[:], dtib_h),
                         (dtob_sb[:], dtob_h), (dtbb_sb[:], dtbb_h),
                         (bpb_sb[:], bpb_h), (cpb_sb[:], cpb_h),
                         (ck_sb[:], ck_h), (cb_sb[:], cb_h),
                         (a_sb[:], alog_h), (dd_sb[:], dd_h), (ns_sb[:], ns_h)):
            resx = tokmerge(resx, s_dma(dst, src))
        p.wtok("vector", resx)
        p.wtok("vector", resb)
        p.emit("vector", lambda e: e.memset(ones1[:], 1.0))
        p.emit("vector", lambda e: e.memset(carry[:], 0.0))
        p.emit("vector", lambda e: e.memset(u_pre[:], 0.0))
        p.emit("vector", lambda e: e.memset(col(KEPS), 1e-5))
        p.emit("vector", lambda e: e.memset(col(KONE), 1.0))
        p.emit("vector", lambda e: e.memset(col(KNHALF), -0.5))
        p.emit("vector", lambda e: e.memset(col(KNEG1), -1.0))
        vinit = p.emit("vector", lambda e: e.tensor_add(dtob_sb[:], dtob_sb[:],
                                                        dtbb_sb[:]), "V")
        p.wait("scalar", "V", vinit)
        p.emit("scalar", lambda e: e.activation(a_sb[:], a_sb[:], AF.Exp))
        ainit = p.emit("scalar", lambda e: e.activation(a_sb[:], a_sb[:], AF.Identity,
                                                        scale=col(KNEG1)), "S")

        uread = [0] * NDT
        ygz_done = 0
        lastW2mm = 0
        lastWs2mm = 0
        outproj_done = 0
        ffn_last_read = 0
        xn_fm_free = 0
        evacT = [0]
        scan_v = 0
        out_tok = [None] * NTG
        sgread = 0
        w2tmp_free = [0, 0]
        ffn_pool_done = 0

        IPW0, W1R0, BB0, CB0 = 16384, 20480, 0, 8192
        ipw_free = [0] * 4
        ring_free = [0] * 4
        ring_slotof = {}
        rctr2 = [0]
        ipw_tok = [None] * 32
        xa_vfree = [0, 0]
        xa_sfree = [0, 0]
        live = {"s_dtr": 0, "w2_ld0": {}, "bball_rd": 0, "bball_pool": 0}
        cd = {"free": 0}
        psab_last = [("V", 0), ("V", 0)]
        xn_fm_free = 0
        def make_ffn(xn_fm, w2_ld, wfev_all, p8s, ygz, ring_gather,
                     w1_tok, ws1_tok, ws1_slot, ring_slotof, nchunk, is_last):
            nonlocal lastW2mm, lastWs2mm, ffn_last_read, sgread, xn_fm_free
            nonlocal ffn_pool_done
            p.wait("tensor", "S", live["s_dtr"])
            for _ in range(4):
                ring_gather()
            actev = [0] * 6
            actw_done = [0] * 6
            w2acc = [0] * 8
            w1mm_done = [0] * NE
            for e_i in range(NE):
                p.wait("tensor", "S", wfev_all)
                p.wait("tensor", "S", cd["free"])
                if e_i > 0:
                    p.wait("tensor", "P", actw_done[5])
                p.emit("tensor", lambda e, e_i=e_i: e.matmul(
                    psC[:], sel_sb[0:NE, e_i * 128:(e_i + 1) * 128], wf[:],
                    start=True, stop=True), "T")
                t_wb = p.n["T"]
                p.wait("scalar", "T", t_wb)
                if e_i > 0:
                    p.wait("scalar", "P", actw_done[5])
                s_wb = p.emit("scalar", lambda e: e.activation(wb_e[:], psC[:], AF.Copy), "S")
                cd["free"] = s_wb
                for m in range(12):
                    ps = psAB[m % 2]
                    sl = ring_slotof[(e_i, m)]
                    if m >= 2:
                        if m - 2 < 6:
                            p.wait("tensor", "S", actev[m - 2])
                        else:
                            p.wait("tensor", "P", actw_done[m - 8])
                    elif e_i > 0:
                        p.wait("tensor", "S", w2acc[6 + m])
                    p.wtok("tensor", w1_tok[e_i][m])
                    for k in range(KD):
                        p.emit("tensor", lambda e, ps=ps, sl=sl, m=m, k=k,
                               xn_fm=xn_fm: e.matmul(
                            ps[:], A[:, W1R0 + sl * 1024 + k * 128:
                                     W1R0 + sl * 1024 + (k + 1) * 128],
                            xn_fm[:, k * 512:(k + 1) * 512],
                            start=(k == 0), stop=(k == KD - 1)), "T" if k == KD - 1 else None)
                    mm = p.n["T"]
                    ring_free[sl] = mm
                    ring_gather()
                    p.wait("scalar", "T", mm)
                    if m < 6:
                        actev[m] = p.emit("scalar", lambda e, ps=ps, m=m: e.activation(
                            act[:, m * 512:(m + 1) * 512], ps[:], AF.Copy), "S")
                    else:
                        p.wait("scalar", "P", sgread)
                        s_sg = p.emit("scalar", lambda e, ps=ps: e.activation(
                            sg_t[:], ps[:], AF.Silu), "S")
                        p.wait("gpsimd", "S", s_sg)
                        p.wait("gpsimd", "S", s_wb)
                        sgread = pmit(lambda e, m=m: e.tensor_tensor(
                            act[:, (m - 6) * 512:(m - 5) * 512],
                            act[:, (m - 6) * 512:(m - 5) * 512], sg_t[:], ALU.mult))
                        actw_done[m - 6] = pmit(lambda e, m=m: e.tensor_tensor(
                            act[:, (m - 6) * 512:(m - 5) * 512],
                            act[:, (m - 6) * 512:(m - 5) * 512], wb_e[:], ALU.mult))
                w1mm_done[e_i] = p.n["T"]
                yield
                p.wtok("tensor", w2_ld[e_i])
                p.wait("tensor", "P", actw_done[5])
                for i in range(8):
                    tg, dh = i // 2, i % 2
                    ps = psAB[i % 2]
                    if i >= 2:
                        p.wait("tensor", "S", w2acc[i - 2])
                    for k in range(6):
                        p.emit("tensor", lambda e, ps=ps, tg=tg, dh=dh, k=k: e.matmul(
                            ps[:], act[:, k * 512 + tg * 128: k * 512 + (tg + 1) * 128],
                            B[:, k * 1024 + dh * 512: k * 1024 + (dh + 1) * 512],
                            start=(k == 0), stop=(k == 5)), "T" if k == 5 else None)
                    mm = p.n["T"]
                    p.wait("scalar", "T", mm)
                    if e_i == 0:
                        if i == 0:
                            p.wait("scalar", "V", ffn_last_read)
                            p.wait("scalar", "P", ffn_pool_done)
                        w2acc[i] = p.emit("scalar", lambda e, ps=ps, tg=tg, dh=dh:
                                          e.activation(
                            ffn_acc[:, tg * D + dh * 512: tg * D + (dh + 1) * 512],
                            ps[:], AF.Copy), "S")
                    else:
                        sl2 = i % 2
                        p.wait("scalar", "P", w2tmp_free[sl2])
                        w2acc[i] = p.emit("scalar", lambda e, ps=ps, sl2=sl2:
                                          e.activation(
                            w2tmp[:, sl2 * 512:(sl2 + 1) * 512], ps[:], AF.Copy), "S")
                        p.wait("gpsimd", "S", w2acc[i])
                        w2tmp_free[sl2] = pmit(lambda e, tg=tg, dh=dh, sl2=sl2:
                                               e.tensor_add(
                            ffn_acc[:, tg * D + dh * 512: tg * D + (dh + 1) * 512],
                            ffn_acc[:, tg * D + dh * 512: tg * D + (dh + 1) * 512],
                            w2tmp[:, sl2 * 512:(sl2 + 1) * 512]))
                lastW2mm = p.n["T"]
                yield
                if e_i + 1 < NE:
                    p.wait("gpsimd", "T", lastW2mm)
                    t2 = {}
                    for k in range(6):
                        t2 = tokmerge(t2, g_dma(B[:, k * 1024:(k + 1) * 1024],
                                                w2t_h[e_i + 1, k * 128:(k + 1) * 128, :]))
                    w2_ld[e_i + 1] = t2

            ws2_ld = {}
            for k in range(6, 10):
                sl = k - 6
                p.wait("gpsimd", "T", ipw_free[sl])
                ws2_ld = tokmerge(ws2_ld,
                                  g_dma(A[:, IPW0 + sl * 1024: IPW0 + (sl + 1) * 1024],
                                        ws2t_h[k * 128:(k + 1) * 128, :]))
            p.wait("gpsimd", "V", ygz_done)
            p.wait("gpsimd", "S", p8_scalar_end)
            p.wait("gpsimd", "T", lastW2mm)
            ws2_ld = tokmerge(ws2_ld, g_dma(xn_tm[:, 0:1024],
                                            ws2t_h[10 * 128:11 * 128, :]))
            ws2_ld = tokmerge(ws2_ld, g_dma(h2b[:, 0:1024],
                                            ws2t_h[11 * 128:12 * 128, :]))
            for k in range(6):
                ws2_ld = tokmerge(ws2_ld, g_dma(B[:, k * 1024:(k + 1) * 1024],
                                                ws2t_h[k * 128:(k + 1) * 128, :]))

            shev = [0] * 24
            for m in range(24):
                ps = psAB[m % 2]
                sl = ws1_slot[m]
                if m % 8 == 7:
                    yield
                if m >= 2:
                    if m - 2 < 12:
                        p.wait("tensor", "S", shev[m - 2])
                    else:
                        p.wait("tensor", "P", shev[m - 2])
                else:
                    p.wait("tensor", "S", w2acc[6 + m])
                p.wtok("tensor", ws1_tok[m])
                for k in range(KD):
                    p.emit("tensor", lambda e, ps=ps, sl=sl, m=m, k=k, xn_fm=xn_fm:
                           e.matmul(
                        ps[:], A[:, W1R0 + sl * 1024 + k * 128:
                                  W1R0 + sl * 1024 + (k + 1) * 128],
                        xn_fm[:, k * 512:(k + 1) * 512],
                        start=(k == 0), stop=(k == KD - 1)), "T" if k == KD - 1 else None)
                mm = p.n["T"]
                ring_free[sl] = mm
                ring_gather()
                p.wait("scalar", "T", mm)
                if m < 12:
                    shev[m] = p.emit("scalar", lambda e, ps=ps, m=m: e.activation(
                        act[:, m * 512:(m + 1) * 512], ps[:], AF.Copy), "S")
                else:
                    p.wait("scalar", "P", sgread)
                    s_sg = p.emit("scalar", lambda e, ps=ps: e.activation(
                        sg_t[:], ps[:], AF.Silu), "S")
                    p.wait("gpsimd", "S", s_sg)
                    shev[m] = sgread = pmit(lambda e, m=m: e.tensor_tensor(
                        act[:, (m - 12) * 512:(m - 11) * 512],
                        act[:, (m - 12) * 512:(m - 11) * 512], sg_t[:], ALU.mult))
            xn_fm_free = p.n["T"]
            p.wtok("tensor", ws2_ld)
            p.wait("tensor", "P", shev[23])
            p.wait("tensor", "S", shev[11])
            wsacc = [0] * 8
            yield
            for i in range(8):
                tg, dh = i // 2, i % 2
                ps = psAB[i % 2]
                if i >= 2:
                    p.wait("tensor", "S", wsacc[i - 2])
                else:
                    p.wait("tensor", "S", w2acc[6 + i])
                for k in range(12):
                    if k < 6:
                        c0 = k * 1024 + dh * 512
                        rhs = B[:, c0: c0 + 512]
                    elif k < 10:
                        c0 = IPW0 + (k - 6) * 1024 + dh * 512
                        rhs = A[:, c0: c0 + 512]
                    elif k == 10:
                        rhs = xn_tm[:, dh * 512: dh * 512 + 512]
                    else:
                        rhs = h2b[:, dh * 512: dh * 512 + 512]
                    p.emit("tensor", lambda e, ps=ps, tg=tg, rhs=rhs, k=k: e.matmul(
                        ps[:], act[:, k * 512 + tg * 128: k * 512 + (tg + 1) * 128],
                        rhs, start=(k == 0), stop=(k == 11)), "T" if k == 11 else None)
                mm = p.n["T"]
                p.wait("scalar", "T", mm)
                sl2 = i % 2
                p.wait("scalar", "P", w2tmp_free[sl2])
                wsacc[i] = p.emit("scalar", lambda e, ps=ps, sl2=sl2: e.activation(
                    w2tmp[:, sl2 * 512:(sl2 + 1) * 512], ps[:], AF.Copy), "S")
                p.wait("gpsimd", "S", wsacc[i])
                w2tmp_free[sl2] = pmit(lambda e, tg=tg, dh=dh, sl2=sl2: e.tensor_add(
                    ffn_acc[:, tg * D + dh * 512: tg * D + (dh + 1) * 512],
                    ffn_acc[:, tg * D + dh * 512: tg * D + (dh + 1) * 512],
                    w2tmp[:, sl2 * 512:(sl2 + 1) * 512]))
            ffn_pool_done = p.n["P"]
            lastWs2mm = p.n["T"]
            for sl in range(4):
                ipw_free[sl] = lastWs2mm
            if not is_last:
                p.wait("gpsimd", "T", lastWs2mm)
                t2 = {}
                for k in range(6):
                    t2 = tokmerge(t2, g_dma(B[:, k * 1024:(k + 1) * 1024],
                                            w2t_h[0, k * 128:(k + 1) * 128, :]))
                live["w2_ld0"] = t2


        prev = None
        for c in range(NCH + 1):
            if prev is not None:
                rows0 = prev["rows0"]
                xn_fm = prev["xn_fm"]
                attn_add = [0] * NTG
                attn_add2 = [0] * 8
                op9 = [[0, 0] for _ in range(NTG)]
                wavebank = [psC, psD, psE, psF]
                p.wait("tensor", "S", p8_scalar_end)
                p.wait("tensor", "S", cd["free"])
                p.wait("tensor", "V", ygz_done)
                xl9 = [None] * NTG
                for wv in range(2):
                    gs = (0, 1) if wv == 0 else (2, 3)
                    for g in gs:
                        sl = g % 2
                        p.wait("sync", "V", xa_vfree[sl])
                        p.wait("sync", "S", xa_sfree[sl])
                        xl9[g] = s_dma(XAp[sl][:],
                                       x_h[rows0 + g * 128: rows0 + (g + 1) * 128, :])
                    if wv == 1:
                        p.wait("tensor", "V", attn_add[1])
                    opw_tok = [None] * NDT
                    opw_sl = [0] * NDT

                    def opw_gather(k, opw_tok=opw_tok, opw_sl=opw_sl):
                        sl = rctr2[0] % 4
                        rctr2[0] += 1
                        opw_sl[k] = sl
                        p.wait("gpsimd", "T", ring_free[sl])
                        opw_tok[k] = g_dma(
                            A[:, W1R0 + sl * 1024: W1R0 + (sl + 1) * 1024],
                            opw_h[k * 128:(k + 1) * 128, :])
                    for k in range(4):
                        opw_gather(k)
                    for k in range(NDT):
                        p.wtok("tensor", opw_tok[k])
                        sl = opw_sl[k]
                        for gi, g in enumerate(gs):
                            for half in range(2):
                                ps = wavebank[gi * 2 + half]
                                p.emit("tensor", lambda e, ps=ps, g=g, k=k, sl=sl,
                                       half=half: e.matmul(
                                    ps[:],
                                    szygz[:, k * 512 + g * 128: k * 512 + (g + 1) * 128],
                                    A[:, W1R0 + sl * 1024 + half * 512:
                                      W1R0 + sl * 1024 + half * 512 + 512],
                                    start=(k == 0), stop=False),
                                    "T" if gi == 1 and half == 1 else None)
                        ring_free[sl] = p.n["T"]
                        if k + 4 < NDT:
                            opw_gather(k + 4)
                    for gi, g in enumerate(gs):
                        for half in range(2):
                            ps = wavebank[gi * 2 + half]
                            op9[g][half] = p.emit(
                                "tensor", lambda e, ps=ps, half=half: e.matmul(
                                    ps[:], ones1[0:1, 0:128],
                                    opb_sb[0:1, half * 512: half * 512 + 512],
                                    start=False, stop=True), "T")
                    for gi, g in enumerate(gs):
                        sl = g % 2
                        xg = xsb[:, g * D:(g + 1) * D]
                        p.wait("scalar", "T", op9[g][0])
                        p.wtok("scalar", xl9[g])
                        s_xin = p.emit("scalar", lambda e, g=g, sl=sl: e.activation(
                            S1[:], XAp[sl][:], AF.Identity, bias=col(NEGMI + g),
                            scale=col(INV + g)), "S")
                        p.wait("vector", "S", s_xin)
                        for half in range(2):
                            ps = wavebank[gi * 2 + half]
                            p.wait("vector", "T", op9[g][half])
                            attn_add2[2 * g + half] = p.emit(
                                "vector", lambda e, ps=ps, half=half: e.tensor_add(
                                    S1[:, half * 512:(half + 1) * 512], ps[:],
                                    S1[:, half * 512:(half + 1) * 512]), "V")
                        attn_add[g] = p.n["V"]
                        p.emit("vector", lambda e: e.reduce_sum(col(C1), S1[:], axis=AX),
                               "V")
                        p.wait("scalar", "V", attn_add[g] + 1)
                        s_sq = p.emit("scalar", lambda e: e.activation(
                            xn_tm[:], S1[:], AF.Square, accum_out=col(C2)), "S")
                        p.wait("vector", "S", s_sq)
                        v_i2 = lnstats(C6, C7, C1, C2)
                        p.emit("vector", lambda e: e.tensor_scalar(
                            S1[:], S1[:], col(C6), col(C7), ALU.mult, ALU.add))
                        p.wtok("vector", out_tok[g] if c > 0 else {})
                        x2v = p.emit("vector", lambda e, xg=xg, sl=sl: e.tensor_add(
                            xg, XAp[sl][:], S1[:]), "V")
                        xa_vfree[sl] = x2v
                        xa_sfree[sl] = s_xin
                        p.emit("vector", lambda e, xg=xg: e.reduce_sum(col(C1), xg,
                                                                       axis=AX), "V")
                        p.wait("scalar", "V", x2v + 1)
                        s_sq3 = p.emit("scalar", lambda e, xg=xg: e.activation(
                            xn_tm[:], xg, AF.Square, accum_out=col(C2)), "S")
                        p.wait("vector", "S", s_sq3)
                        v_i3 = lnstats(C6, C7, C1, C2)
                        p.wait("scalar", "V", v_i3)
                        s_xn3 = p.emit("scalar", lambda e, xg=xg: e.activation(
                            xn_tm[:], xg, AF.Identity, bias=col(C7), scale=col(C6)), "S")
                        ps = psG
                        p.wait("tensor", "S", s_xn3)
                        p.wait("tensor", "S", evacT[0])
                        for d in range(KD):
                            p.emit("tensor", lambda e, ps=ps, d=d: e.transpose(
                                ps[:, d * 128:(d + 1) * 128],
                                xn_tm[:, d * 128:(d + 1) * 128],
                                id_bf[:]), "T" if d == KD - 1 else None)
                        t_tr3 = p.n["T"]
                        p.wait("scalar", "T", t_tr3)
                        p.emit("scalar", lambda e, ps=ps, g=g, xn_fm=xn_fm: e.activation(
                            bass.AP(xn_fm, g * 128, [[4096, 128], [512, 8], [1, 128]]),
                            ps[:, 0:1024], AF.Copy), "S")
                        evacT[0] = p.n["S"]
                outproj_done = op9[NTG - 1][1]
                xn3_ready = p.n["S"]

                p.wait("tensor", "S", xn3_ready)
                p.wait("tensor", "S", p8_scalar_end)
                p.wait("tensor", "V", ygz_done)
                gate_ps = [psE, psF, psH, psA]
                for g in range(NTG):
                    for k in range(KD):
                        p.emit("tensor", lambda e, g=g, k=k, xn_fm=xn_fm: e.matmul(
                            gate_ps[g][:, 0:NE],
                            xn_fm[:, k * 512 + g * 128: k * 512 + (g + 1) * 128],
                            gw_sb[:, k * NE:(k + 1) * NE],
                            start=(k == 0), stop=(k == KD - 1)),
                            "T" if k == KD - 1 else None)
                    t_gg = p.n["T"]
                    p.wait("scalar", "T", t_gg)
                    p.emit("scalar", lambda e, g=g: e.activation(
                        gsc[:, g * NE:(g + 1) * NE], gate_ps[g][:, 0:NE], AF.Copy), "S")
                s_gev = p.n["S"]
                p.wait("vector", "S", s_gev)
                t_g = p.n["T"]
                gsc3 = bass.AP(gsc, 0, [[32, 128], [8, 4], [1, 8]])
                gmask3 = bass.AP(gmask, 0, [[32, 128], [8, 4], [1, 8]])
                m1_bc = bass.AP(st, 12, [[24, 128], [1, 4], [0, 8]])
                m2_bc = bass.AP(st, 8, [[24, 128], [1, 4], [0, 8]])
                p.wait("vector", "T", t_g)
                p.emit("vector", lambda e: e.reduce_max(st[:, 12:16], gsc3, axis=AX))
                p.emit("vector", lambda e: e.tensor_tensor(gsc3, gsc3, m1_bc, ALU.subtract),
                       "V")
                v_sub = p.n["V"]
                p.wait("scalar", "V", v_sub)
                s_e = p.emit("scalar", lambda e: e.activation(gsc[:], gsc[:], AF.Exp), "S")
                p.wait("vector", "S", s_e)
                p.emit("vector", lambda e: e.reduce_sum(st[:, 8:12], gsc3, axis=AX))
                p.emit("vector", lambda e: e.reciprocal(st[:, 8:12], st[:, 8:12]))
                p.emit("vector", lambda e: e.tensor_tensor(gsc3, gsc3, m2_bc, ALU.mult))
                p.emit("vector", lambda e: e.reduce_max(st[:, 12:16], gsc3, axis=AX))
                p.emit("vector", lambda e: e.tensor_tensor(gmask3, gsc3, m1_bc, ALU.is_ge))
                p.emit("vector", lambda e: e.scalar_tensor_tensor(
                    gmask[:], gmask[:], -1e30, gsc[:], ALU.mult, ALU.add))
                p.emit("vector", lambda e: e.reduce_max(st[:, 8:12], gmask3, axis=AX))
                p.emit("vector", lambda e: e.tensor_tensor(gmask3, gsc3, m2_bc, ALU.is_ge))
                gdone_all = p.emit("vector", lambda e: e.tensor_tensor(
                    w_tm[:, 0:32], gsc[:], gmask[:], ALU.mult), "V")
                p.wait("tensor", "V", gdone_all)
                p.wait("tensor", "S", evacT[0])
                for g in range(NTG):
                    p.emit("tensor", lambda e, g=g: e.transpose(
                        psG[0:NE, g * 128:(g + 1) * 128], w_tm[:, g * NE:(g + 1) * NE],
                        id_bf[:]), "T" if g == NTG - 1 else None)
                t_wt = p.n["T"]
                p.wait("scalar", "T", t_wt)
                wfev_all = p.emit("scalar", lambda e: e.activation(
                    wf[:], psG[0:NE, 0:512], AF.Copy), "S")
                evacT[0] = p.n["S"]

                ffn_it = make_ffn(prev["xn_fm"], prev["w2_ld"], wfev_all,
                                  prev["p8s"], prev["ygz"], prev["ring_gather"],
                                  prev["w1_tok"], prev["ws1_tok"], prev["ws1_slot"],
                                  prev["ring_slotof"], c, c == NCH)
            else:
                ffn_it = iter(())
            if c < NCH:
                rows0 = c * TC
                xn_fm = xn_fmA if c % 2 == 0 else xn_fmB
                def ipw_gather(m):
                    sl = m % 4
                    p.wait("gpsimd", "T", ipw_free[sl])
                    ipw_tok[m] = p.dma("gpsimd",
                                       bass.AP(A, IPW0 + sl * 1024,
                                               [[24576, 128], [128, 8], [1, 128]]),
                                       bass.AP(ipw_h.tensor, m * 128,
                                               [[4096, 128], [524288, 8], [1, 128]]))
                for m in range(4):
                    ipw_gather(m)

                xload = [None] * NTG
                for g in (0, 1):
                    sl = g % 2
                    p.wait("sync", "V", xa_vfree[sl])
                    p.wait("sync", "S", xa_sfree[sl])
                    xload[g] = s_dma(XAp[sl][:],
                                     x_h[rows0 + g * 128: rows0 + (g + 1) * 128, :])
                for g in range(NTG):
                    xg = XAp[g % 2][:]
                    p.wtok("vector", xload[g])
                    p.emit("vector", lambda e, xg=xg: e.reduce_sum(col(C1), xg, axis=AX), "V")
                    p.wtok("scalar", xload[g])
                    s_sq = p.emit("scalar", lambda e, xg=xg: e.activation(
                        S1[:], xg, AF.Square, accum_out=col(C2)), "S")
                    p.wait("vector", "S", s_sq)
                    v_i = lnstats(INV + g, NEGMI + g, C1, C2)
                    p.wait("scalar", "V", v_i)
                    s_xin1 = p.emit("scalar", lambda e, xg=xg, g=g: e.activation(
                        S1[:], xg, AF.Identity, bias=col(NEGMI + g), scale=col(INV + g)), "S")
                    xa_vfree[g % 2] = p.n["V"]
                    xa_sfree[g % 2] = s_xin1
                    if g + 2 < NTG:
                        sl2 = (g + 2) % 2
                        p.wait("sync", "V", xa_vfree[sl2])
                        p.wait("sync", "S", xa_sfree[sl2])
                        xload[g + 2] = s_dma(
                            XAp[sl2][:],
                            x_h[rows0 + (g + 2) * 128: rows0 + (g + 3) * 128, :])
                    s_rms = p.emit("scalar", lambda e: e.activation(
                        xn_tm[:], S1[:], AF.Square, accum_out=col(C6)), "S")
                    p.wait("vector", "S", s_rms)
                    v_r = p.emit("vector", lambda e: e.tensor_scalar(
                        col(C7), col(C6), 1.0 / D, 1e-6, ALU.mult, ALU.add), "V")
                    p.wait("scalar", "V", v_r)
                    p.emit("scalar", lambda e: e.activation(col(C7), col(C7), AF.Ln))
                    p.emit("scalar", lambda e: e.activation(col(C7), col(C7), AF.Exp,
                                                            scale=col(KNHALF)))
                    s_xn = p.emit("scalar", lambda e: e.activation(
                        xn_tm[:], S1[:], AF.Copy, scale=col(C7)), "S")
                    ps = psG
                    p.wait("tensor", "S", s_xn)
                    p.wait("tensor", "S", evacT[0])
                    for d in range(KD):
                        p.emit("tensor", lambda e, ps=ps, d=d: e.transpose(
                            ps[:, d * 128:(d + 1) * 128], xn_tm[:, d * 128:(d + 1) * 128],
                            id_bf[:]), "T" if d == KD - 1 else None)
                    t_tr = p.n["T"]
                    p.wait("scalar", "T", t_tr)
                    if g == 0:
                        p.wait("scalar", "T", xn_fm_free)
                    for d in range(KD):
                        p.emit("scalar", lambda e, ps=ps, d=d, g=g, xn_fm=xn_fm: e.activation(
                            xn_fm[:, d * 512 + g * 128: d * 512 + (g + 1) * 128],
                            ps[:, d * 128:(d + 1) * 128], AF.Copy, scale=ns_sb[:, d:d + 1]), "S")
                    evacT[0] = p.n["S"]
                xn1_ready = p.n["S"]

                p.wait("tensor", "S", xn1_ready)
                p.wait("tensor", "S", ainit)
                evac3 = [0] * 32
                p.wait("tensor", "V", attn_add[NTG - 1] if c > 0 else 0)
                p.wait("tensor", psab_last[0][0], psab_last[0][1])
                p.wait("tensor", psab_last[1][0], psab_last[1][1])
                live["evac3"] = evac3
                for m in range(32):
                    ps = psABCD[m % 4]
                    sl = m % 4
                    if m >= 4:
                        p.wait("tensor", "S", evac3[m - 4])
                    p.wtok("tensor", ipw_tok[m])
                    for k in range(KD):
                        p.emit("tensor", lambda e, ps=ps, sl=sl, m=m, k=k, xn_fm=xn_fm:
                               e.matmul(
                            ps[:], A[:, IPW0 + sl * 1024 + k * 128: IPW0 + sl * 1024 + (k + 1) * 128],
                            xn_fm[:, k * 512:(k + 1) * 512],
                            start=(k == 0), stop=(k == KD - 1)), "T" if k == KD - 1 else None)
                    mm = p.n["T"]
                    ipw_free[sl] = mm
                    if m + 4 < 32:
                        ipw_gather(m + 4)
                    p.wait("scalar", "T", mm)
                    if m < NDT:
                        p.wait("scalar", "V", uread[m])
                        p.emit("scalar", lambda e, ps=ps, m=m: e.activation(
                            u_pre[:, m * 516 + 4: m * 516 + 516], ps[:], AF.Identity,
                            bias=ipb_sb[:, m:m + 1]))
                    else:
                        p.wait("scalar", "V", ygz_done)
                        p.wait("scalar", "T", outproj_done)
                        p.emit("scalar", lambda e, ps=ps, m=m: e.activation(
                            szygz[:, (m - NDT) * 512: (m - NDT + 1) * 512], ps[:], AF.Silu,
                            bias=ipb_sb[:, m:m + 1]))
                    evac3[m] = p.n["S"]

                if c == 0:
                    t2 = {}
                    for k in range(6):
                        t2 = tokmerge(t2, g_dma(B[:, k * 1024:(k + 1) * 1024],
                                                w2t_h[0, k * 128:(k + 1) * 128, :]))
                    live["w2_ld0"] = t2
                p.wait("gpsimd", "T", ipw_free[0])
                dtow_ld = g_dma(A[:, IPW0: IPW0 + 1024], dtow_h[:, :])
                w2_ld = [None] * NE
                w2_ld[0] = live["w2_ld0"]
                ring_slotof = {}
                w1_tok = [[None] * 12 for _ in range(NE)]
                ws1_tok = [None] * 24
                ws1_slot = [0] * 24
                gq = [("w1", e, m) for e in range(NE) for m in range(12)] + \
                     [("ws1", 0, m) for m in range(24)]
                gq_next = [0]

                def ring_gather(gq=gq, gq_next=gq_next, w1_tok=w1_tok,
                                ws1_tok=ws1_tok, ws1_slot=ws1_slot,
                                ring_slotof=ring_slotof):
                    if gq_next[0] >= len(gq):
                        return
                    kind, e, m = gq[gq_next[0]]
                    gq_next[0] += 1
                    sl = rctr2[0] % 4
                    rctr2[0] += 1
                    p.wait("gpsimd", "T", ring_free[sl])
                    if kind == "w1":
                        tok = p.dma("gpsimd",
                                    bass.AP(A, W1R0 + sl * 1024,
                                            [[24576, 128], [128, 8], [1, 128]]),
                                    bass.AP(w1t_h.tensor, e * 1572864 + m * 128,
                                            [[1536, 128], [196608, 8], [1, 128]]))
                        w1_tok[e][m] = tok
                        ring_slotof[(e, m)] = sl
                    else:
                        tok = p.dma("gpsimd",
                                    bass.AP(A, W1R0 + sl * 1024,
                                            [[24576, 128], [128, 8], [1, 128]]),
                                    bass.AP(ws1t_h.tensor, m * 128,
                                            [[3072, 128], [393216, 8], [1, 128]]))
                        ws1_tok[m] = tok
                        ws1_slot[m] = sl

                usil = [0] * NDT
                for d in range(NDT):
                    ub = d * 516
                    p.wait("vector", "S", evac3[d])
                    p.emit("vector", lambda e, d=d, ub=ub: e.tensor_copy(
                        hst[:, d * 3:(d + 1) * 3], u_pre[:, ub + 513: ub + 516]))
                    p.emit("vector", lambda e, ub=ub, d=d: e.tensor_scalar(
                        yac2[:, 0:512], u_pre[:, ub + 1: ub + 513], ck_sb[:, d * 4: d * 4 + 1],
                        None, ALU.mult))
                    v_cv = None
                    for j in (1, 2, 3):
                        v_cv = p.emit("vector", lambda e, ub=ub, d=d, j=j: e.scalar_tensor_tensor(
                            yac2[:, 0:512], u_pre[:, ub + 1 + j: ub + 513 + j],
                            ck_sb[:, d * 4 + j: d * 4 + j + 1], yac2[:, 0:512],
                            ALU.mult, ALU.add), "V")
                    p.wait("scalar", "V", v_cv)
                    usil[d] = p.emit("scalar", lambda e, ub=ub, d=d: e.activation(
                        u_pre[:, ub + 4: ub + 516], yac2[:, 0:512], AF.Silu,
                        bias=cb_sb[:, d:d + 1]), "S")
                    p.wait("vector", "S", usil[d])
                    p.emit("vector", lambda e, ub=ub, d=d: e.tensor_copy(
                        u_pre[:, ub + 1: ub + 4], hst[:, d * 3:(d + 1) * 3]), "V")
                conv_done_v = p.n["V"]

                p.wait("tensor", "S", usil[NDT - 1])
                p.wait("tensor", "S", evac3[28])
                for half in range(2):
                    for k in range(NDT):
                        p.emit("tensor", lambda e, k=k, half=half: e.matmul(
                            psA[half * DTR:(half + 1) * DTR, :],
                            dtiw_sb[:, k * DTR:(k + 1) * DTR],
                            u_pre[:, k * 516 + 4: k * 516 + 516],
                            start=(k == 0), stop=(k == NDT - 1)),
                            "T" if k == NDT - 1 else None)
                t_dtr = p.n["T"]
                p.wait("scalar", "T", t_dtr)
                s_dtr = p.emit("scalar", lambda e: e.activation(
                    dtr_sb[:], psA[:], AF.Identity, bias=dtib_sb[:, 0:1]), "S")

                p.wait("tensor", "S", evac3[31])
                bc_ev = []
                for (w_sb, bias_sb, dst, psx) in ((bpw_sb, bpb_sb, b_sb, psE),
                                                  (cpw_sb, cpb_sb, c_sb, psF)):
                    for k in range(NDT):
                        p.emit("tensor", lambda e, w_sb=w_sb, k=k, psx=psx: e.matmul(
                            psx[0:DS, :], w_sb[:, k * DS:(k + 1) * DS],
                            u_pre[:, k * 516 + 4: k * 516 + 516],
                            start=(k == 0), stop=(k == NDT - 1)), "T" if k == NDT - 1 else None)
                    t_bc = p.n["T"]
                    p.wait("scalar", "T", t_bc)
                    bc_ev.append(p.emit("scalar", lambda e, dst=dst, bias_sb=bias_sb, psx=psx:
                                        e.activation(dst[:], psx[0:DS, :], AF.Tanh,
                                                     bias=bias_sb[:, 0:1]), "S"))
                s_bc = p.n["S"]

                live["s_dtr"] = s_dtr
                sp_ev = [0] * NDT
                last_vclip = 0
                cd["free"] = max(cd["free"], s_dtr)
                p.wait("scalar", "V", ffn_last_read)
                p.wait("scalar", "T", lastWs2mm)
                p.wait("tensor", "S", s_bc)
                p.wait("tensor", "S", cd["free"])
                p.wait("tensor", "V", attn_add[NTG - 1] if c > 0 else 0)
                p.wait("scalar", "V", live["bball_rd"])
                p.wait("scalar", "P", live["bball_pool"])
                fill_ev = [0] * DS
                fbanks = [(psC, psD), (psE, psF)]
                for s in range(DS):
                    pb, pc = fbanks[s % 2]
                    if s >= 2:
                        p.wait("tensor", "S", fill_ev[s - 2])
                    p.emit("tensor", lambda e, s=s, pb=pb: e.matmul(
                        pb[:], sel_sb[:, s * 128:(s + 1) * 128], b_sb[:],
                        start=True, stop=True))
                    p.emit("tensor", lambda e, s=s, pc=pc: e.matmul(
                        pc[:], sel_sb[:, s * 128:(s + 1) * 128], c_sb[:],
                        start=True, stop=True), "T")
                    t_f = p.n["T"]
                    p.wait("scalar", "T", t_f)
                    p.emit("scalar", lambda e, s=s, pb=pb: e.activation(
                        A[:, BB0 + s * 512: BB0 + (s + 1) * 512], pb[:], AF.Copy))
                    fill_ev[s] = p.emit("scalar", lambda e, s=s, pc=pc: e.activation(
                        A[:, CB0 + s * 512: CB0 + (s + 1) * 512], pc[:], AF.Copy), "S")
                cd["free"] = fill_ev[DS - 1]
                scan_hist = [0, 0, 0, 0]
                da_ps4 = [psF, psH, psE, psD]
                da_ev = [0, 0, 0, 0]
                h2p = [h2, h2b]
                yadd_p = [0, 0]
                yac2_free = 0
                for d in range(NDT):
                    ub = d * 516
                    p.wait("tensor", "S", s_dtr)
                    if d > 0:
                        p.wait("tensor", "S", da_ev[2])
                        p.wait("tensor", "V", scan_hist[2])
                    else:
                        p.wait("tensor", "S", cd["free"])
                    if d == 0:
                        p.wtok("tensor", dtow_ld)
                    p.emit("tensor", lambda e, d=d: e.matmul(
                        psE[:], A[(d // 8) * 64:(d // 8) * 64 + 64,
                                  IPW0 + (d % 8) * 128: IPW0 + (d % 8 + 1) * 128],
                        dtr_sb[(d // 8) * 64:(d // 8) * 64 + 64, :],
                        start=True, stop=True), "T")
                    t_dt = p.n["T"]
                    p.wait("scalar", "T", t_dt)
                    p.wait("scalar", "V", conv_done_v if d == 0 else last_vclip)
                    p.emit("scalar", lambda e, d=d: e.activation(
                        dt_t[:], psE[:], AF.Exp, bias=dtob_sb[:, d:d + 1]))
                    sp_ev[d] = p.emit("scalar", lambda e: e.activation(
                        dt_t[:], dt_t[:], AF.Ln, bias=col(KONE)), "S")
                    p.wait("vector", "S", sp_ev[d])
                    p.emit("vector", lambda e: e.tensor_scalar(dt_t[:], dt_t[:], 0.001, 0.1,
                                                               ALU.max, ALU.min))
                    v_clip = p.emit("vector", lambda e, ub=ub: e.tensor_tensor(
                        w_t[:], dt_t[:], u_pre[:, ub + 4: ub + 516], ALU.mult), "V")
                    uread[d] = v_clip
                    last_vclip = v_clip
                    w_rep = bass.AP(w_t, 0, [[512, 128], [0, 2], [1, 512]])
                    h2_pool_rd = [0, 0]
                    for grp in range(8):
                        s0 = 2 * grp
                        hcur = h2p[grp % 2]
                        if h2_pool_rd[grp % 2]:
                            p.wait("vector", "P", h2_pool_rd[grp % 2])
                        for j in (0, 1):
                            sI = s0 + j
                            bk = sI % 4
                            p.wait("scalar", "V", max(v_clip, scan_hist[bk]))
                            da_ev[bk] = p.emit("scalar", lambda e, d=d, sI=sI, bk=bk:
                                               e.activation(
                                da_ps4[bk][:], dt_t[:], AF.Exp,
                                scale=a_sb[:, d * DS + sI: d * DS + sI + 1]), "S")
                        s_da = p.n["S"]
                        if d == 0:
                            p.wait("vector", "S", fill_ev[s0 + 1])
                        p.emit("vector", lambda e, w_rep=w_rep, s0=s0: e.tensor_tensor(
                            db2[:], w_rep,
                            A[:, BB0 + s0 * 512: BB0 + (s0 + 2) * 512], ALU.mult), "V")
                        p.wait("vector", "S", s_da)
                        for j in (0, 1):
                            sI = s0 + j
                            bk = sI % 4
                            scan_hist[bk] = p.emit(
                                "vector", lambda e, d=d, sI=sI, j=j, bk=bk,
                                hcur=hcur: e.tensor_tensor_scan(
                                    hcur[:, j * 512:(j + 1) * 512], da_ps4[bk][:],
                                    db2[:, j * 512:(j + 1) * 512],
                                    carry[:, d * DS + sI: d * DS + sI + 1],
                                    ALU.mult, ALU.add), "V")
                        p.emit("vector", lambda e, d=d, s0=s0, hcur=hcur: e.tensor_copy(
                            carry[:, d * DS + s0: d * DS + s0 + 2],
                            bass.AP(hcur, 511, [[1024, 128], [512, 2]])))
                        if grp == 0:
                            p.emit("vector", lambda e, s0=s0, hcur=hcur: e.tensor_tensor(
                                yac2[:], hcur[:],
                                A[:, CB0 + s0 * 512: CB0 + (s0 + 2) * 512],
                                ALU.mult), "V")
                        else:
                            p.emit("vector", lambda e, s0=s0, hcur=hcur: e.tensor_tensor(
                                db2[:], hcur[:],
                                A[:, CB0 + s0 * 512: CB0 + (s0 + 2) * 512],
                                ALU.mult), "V")
                            p.emit("vector", lambda e: e.tensor_add(yac2[:], yac2[:],
                                                                    db2[:]), "V")
                    p.emit("vector", lambda e: e.tensor_add(
                        yac2[:, 0:512], yac2[:, 0:512], yac2[:, 512:1024]))
                    p.emit("vector", lambda e, ub=ub, d=d: e.scalar_tensor_tensor(
                        yac2[:, 0:512], u_pre[:, ub + 4: ub + 516], dd_sb[:, d:d + 1],
                        yac2[:, 0:512], ALU.mult, ALU.add))
                    yac2_free = p.emit("vector", lambda e, d=d: e.tensor_tensor(
                        szygz[:, d * 512:(d + 1) * 512], yac2[:, 0:512],
                        szygz[:, d * 512:(d + 1) * 512], ALU.mult), "V")
                    next(ffn_it, None)
                ygz_done = p.n["V"]
                live["bball_rd"] = ygz_done
                live["bball_pool"] = p.n["P"]
                p8_scalar_end = p.n["S"]
                ipw_free[0] = p.n["T"]

            for _ in ffn_it:
                pass
            xn_fm_free = p.n["T"]
            if prev is not None:
                rows0 = prev["rows0"]
                p.wait("vector", "P", ffn_pool_done)
                for g in range(NTG):
                    xg = xsb[:, g * D:(g + 1) * D]
                    fg = ffn_acc[:, g * D:(g + 1) * D]
                    p.emit("vector", lambda e, fg=fg: e.reduce_sum(col(C1), fg, axis=AX), "V")
                    v_s1 = p.n["V"]
                    p.wait("scalar", "V", v_s1)
                    s_sq = p.emit("scalar", lambda e, fg=fg: e.activation(
                        S1[:], fg, AF.Square, accum_out=col(C2)), "S")
                    p.wait("vector", "S", s_sq)
                    v_i4 = lnstats(C6, C7, C1, C2)
                    p.wait("scalar", "V", v_i4)
                    s_ln4 = p.emit("scalar", lambda e, fg=fg: e.activation(
                        S1[:], fg, AF.Identity, bias=col(C7), scale=col(C6)), "S")
                    p.wait("vector", "S", s_ln4)
                    x3v = p.emit("vector", lambda e, xg=xg: e.tensor_add(xg, xg, S1[:]), "V")
                    p.wait("sync", "V", x3v)
                    out_tok[g] = s_dma(out_h[rows0 + g * 128: rows0 + (g + 1) * 128, :], xg)
                ffn_last_read = p.n["V"]
            if c < NCH:
                prev = dict(rows0=c * TC, xn_fm=xn_fm, w2_ld=w2_ld,
                            p8s=p8_scalar_end, ygz=ygz_done,
                            ring_gather=ring_gather, w1_tok=w1_tok,
                            ws1_tok=ws1_tok, ws1_slot=ws1_slot,
                            ring_slotof=ring_slotof)
            else:
                prev = None


        final_sync = {lane: p.n[lane] for lane in XLANES}

        semmap = {k: sem[k] for k in SEMKEYS}
        block = ec(nc.Block())

        def play(engname, eng):
            selfkey = {"vector": "V", "scalar": "S"}.get(engname)
            cnt = 0
            for op in p.ops[engname]:
                if op[0] == "wait":
                    _, key, val = op
                    if key == selfkey and val >= cnt:
                        continue
                    eng.wait_ge(semmap[key], val)
                else:
                    fn, inckey = op
                    if selfkey is not None and cnt > 0:
                        eng.wait_ge(semmap[selfkey], cnt)
                    ins = fn(eng)
                    if inckey is not None:
                        ins.then_inc(semmap[inckey], 16 if len(inckey) > 1 else 1)
                        cnt += 1

        @block.tensor
        def _(e):
            play("tensor", e)

        @block.vector
        def _(e):
            play("vector", e)

        @block.scalar
        def _(e):
            play("scalar", e)

        @block.sync
        def _(e):
            play("sync", e)
            for lane, val in final_sync.items():
                if val:
                    e.wait_ge(sem[lane], val)

        @block.gpsimd
        def _(e):
            play("gpsimd", e)

    return nc


_NC_CACHE = None


def kernel(**inputs):
    global _NC_CACHE
    ii = {k: np.ascontiguousarray(np.asarray(v, np.float32)) for k, v in inputs.items()}

    def fm(v, ncols):
        v = v.reshape(-1, 128, ncols)
        return np.ascontiguousarray(v.transpose(1, 0, 2).reshape(128, -1))

    shared = {
        "ipw": ii["L0_attn_in_proj_w"],
        "ipb": np.ascontiguousarray(ii["L0_attn_in_proj_b"].reshape(32, 128).T),
        "opw": ii["L0_attn_out_proj_w"],
        "opbr": ii["L0_attn_out_proj_b"].reshape(1, D),
        "dtiw": fm(ii["L0_attn_dt_in_w"], DTR),
        "dtib": np.tile(ii["L0_attn_dt_in_b"].reshape(DTR, 1), (2, 1)),
        "dtow": np.concatenate([ii["L0_attn_dt_out_w"][:, 0:1024],
                                ii["L0_attn_dt_out_w"][:, 1024:2048]], axis=0),
        "dtob": np.ascontiguousarray(ii["L0_attn_dt_out_b"].reshape(NDT, 128).T),
        "dtbb": np.ascontiguousarray(ii["L0_attn_dt_bias"].reshape(NDT, 128).T),
        "bpw": fm(ii["L0_attn_b_proj_w"], DS),
        "bpb": ii["L0_attn_b_proj_b"].reshape(DS, 1),
        "cpw": fm(ii["L0_attn_c_proj_w"], DS),
        "cpb": ii["L0_attn_c_proj_b"].reshape(DS, 1),
        "ck": fm(np.ascontiguousarray(ii["L0_attn_conv_kernel"].T), 4),
        "cb": np.ascontiguousarray(ii["L0_attn_conv_bias"].reshape(NDT, 128).T),
        "alog": fm(ii["L0_attn_a_log"], DS),
        "dd": np.ascontiguousarray(ii["L0_attn_d"].reshape(NDT, 128).T),
        "ns": np.ascontiguousarray(ii["L0_attn_norm_scale"].reshape(KD, 128).T),
        "gw": fm(np.ascontiguousarray(ii["L0_ffn_gate_w"].T), NE),
        "w1t": np.ascontiguousarray(ii["L0_ffn_W1"].transpose(0, 2, 1)),
        "w2t": np.ascontiguousarray(ii["L0_ffn_W2"].transpose(0, 2, 1)),
        "ws1t": np.ascontiguousarray(ii["L0_ffn_Ws1"].T),
        "ws2t": np.ascontiguousarray(ii["L0_ffn_Ws2"].T),
        "ident": np.eye(128, dtype=np.float32),
        "sel": np.kron(np.eye(DS, dtype=np.float32), np.ones((1, 128), np.float32)),
    }
    x = ii["L0_attn_x"]
    in_maps = []
    for core in range(8):
        m = dict(shared)
        m["x"] = np.ascontiguousarray(x[core])
        in_maps.append(m)

    if _NC_CACHE is None:
        _NC_CACHE = build_core()
    res = run_bass_kernel_spmd(_NC_CACHE, in_maps, core_ids=list(range(8)))
    out = np.stack([np.asarray(r["out"], np.float32) for r in res.results], axis=0)
    return out
